# version 08
# baseline (speedup 1.0000x reference)
import contextlib
import numpy as np

import concourse.bass as bass
import concourse.mybir as mybir
from concourse.bass_utils import run_bass_kernel_spmd

F32 = mybir.dt.float32
BF16 = mybir.dt.bfloat16
AF = mybir.ActivationFunctionType
OP = mybir.AluOpType
AX = mybir.AxisListType

B, T, DM, DI, S, R = 8, 2048, 2048, 4096, 16, 128
H_TRUE = 5461
NH = 43
HP = NH * 128
NCC = DI // 128
NDM = DM // 128
NK = DM // 128
EPS = 1e-5
T4 = 512
NT4 = T // T4
TL = 256
NTL = T // TL
T4S = 512
NT4S = T // T4S
INV_DM = 1.0 / DM
INV_W = 1.0 / float(H_TRUE * DM)
N_WT = 43

DEFAULT_UPTO = 99


class LS:

    def __init__(self, name):
        self.name = name
        self.items = []

    def on(self, engine, fn, inc=1):
        self.items.append((engine, fn, inc))

    def dma(self, engine, fn):
        self.items.append((engine, fn, 16))

    def run(self, nc):
        total = sum(k for _, _, k in self.items)
        engines = []
        for e, _, _ in self.items:
            if e not in engines:
                engines.append(e)
        with contextlib.ExitStack() as ps:
            sem = ps.enter_context(nc.semaphore(self.name + "_g"))
            blk = ps.enter_context(nc.Block(self.name))
            for eng_name in engines:
                def make_body(eng_name):
                    def body(eng):
                        acc = 0
                        for (e, fn, inc) in self.items:
                            if e == eng_name:
                                if acc > 0:
                                    eng.wait_ge(sem, acc)
                                inst = fn(eng)
                                assert inst is not None, (
                                    f"{self.name}: step on {e} returned None")
                                inst.then_inc(sem, inc)
                            acc += inc
                        eng.wait_ge(sem, total)
                    return body
                getattr(blk, eng_name)(make_body(eng_name))


class PRes:
    __slots__ = ("name", "w", "r")

    def __init__(self, name):
        self.name = name
        self.w = None
        self.r = []


class PStep:
    __slots__ = ("engine", "fn", "inc", "deps", "gidx", "cum", "rres", "wres",
                 "semkey")

    def __init__(self, engine, fn, inc):
        self.engine = engine
        self.fn = fn
        self.inc = inc
        self.deps = set()
        self.gidx = 0
        self.cum = 0
        self.rres = ()
        self.wres = ()
        self.semkey = None


SERIAL_PHASES = set()


class Pipe:

    def __init__(self, name):
        self.name = name
        self.steps = []
        self.serial = name in SERIAL_PHASES

    def res(self, name="r"):
        return PRes(name)

    def step(self, engine, fn, r=(), w=(), extra=(), inc=1):
        s = PStep(engine, fn, inc)
        for rr in r:
            if rr.w is not None:
                s.deps.add(rr.w)
        for rr in w:
            for rd in rr.r:
                s.deps.add(rd)
            if rr.w is not None:
                s.deps.add(rr.w)
        for d in extra:
            if d is not None:
                s.deps.add(d)
        if self.serial and self.steps:
            s.deps.add(self.steps[-1])
        s.gidx = len(self.steps)
        s.rres = tuple(r)
        s.wres = tuple(w)
        self.steps.append(s)
        for rr in r:
            rr.r.append(s)
        for rr in w:
            rr.w = s
            rr.r = []
        return s

    def dma(self, engine, fn, r=(), w=(), extra=()):
        return self.step(engine, fn, r=r, w=w, extra=extra, inc=16)

    def last_per_engine(self):
        seen = {}
        for s in self.steps:
            seen[s.engine] = s
        return list(seen.values())

    def run(self, nc):
        engines = []
        for s in self.steps:
            if s.engine not in engines:
                engines.append(s.engine)
        for s in self.steps:
            if s.inc == 16:
                rs = (s.wres or s.rres)
                assert rs, f"{self.name}: dma step without resources"
                s.semkey = ("dma", id(rs[0]))
            else:
                s.semkey = ("eng", s.engine)
        cum = {}
        for s in self.steps:
            cum[s.semkey] = cum.get(s.semkey, 0) + s.inc
            s.cum = cum[s.semkey]
        totals = dict(cum)
        with contextlib.ExitStack() as ps:
            sems = {k: ps.enter_context(nc.semaphore(f"{self.name}_{i}"))
                    for i, k in enumerate(totals)}
            blk = ps.enter_context(nc.Block(self.name))
            for eng_name in engines:
                def make_body(eng_name):
                    def body(eng):
                        waited = {k: 0 for k in totals}
                        for s in self.steps:
                            if s.engine != eng_name:
                                continue
                            need = {}
                            for d in s.deps:
                                if d.semkey == ("eng", eng_name):
                                    continue
                                if d.engine == eng_name and d.semkey[0] == "eng":
                                    continue
                                need[d.semkey] = max(need.get(d.semkey, 0), d.cum)
                            for k, tgt in sorted(need.items(), key=str):
                                if tgt > waited[k]:
                                    eng.wait_ge(sems[k], tgt)
                                    waited[k] = tgt
                            inst = s.fn(eng)
                            assert inst is not None, (
                                f"{self.name}: step on {s.engine} returned None")
                            inst.then_inc(sems[s.semkey], s.inc)
                        for k in totals:
                            if k != ("eng", eng_name) and waited[k] < totals[k]:
                                eng.wait_ge(sems[k], totals[k])
                    return body
                getattr(blk, eng_name)(make_body(eng_name))


class Env:
    pass


def build(upto=DEFAULT_UPTO, debug=()):
    nc = bass.Bass("TRN2", target_bir_lowering=False,
                   detect_race_conditions=False)
    v = Env()
    v.debug = set(debug)

    v.d_x = nc.dram_tensor("xT", [DM, T], F32, kind="ExternalInput").ap()
    v.d_win = nc.dram_tensor("w_in_t", [64, 128, NK * 128], F32, kind="ExternalInput").ap()
    v.d_cw = nc.dram_tensor("conv_w_t", [128, NCC * 3], F32, kind="ExternalInput").ap()
    v.d_cb = nc.dram_tensor("conv_b_t", [128, NCC], F32, kind="ExternalInput").ap()
    v.d_wx = nc.dram_tensor("w_x_t", [NCC, 128, 160], F32, kind="ExternalInput").ap()
    v.d_wdt = nc.dram_tensor("w_dt_t", [128, DI], F32, kind="ExternalInput").ap()
    v.d_bdt = nc.dram_tensor("b_dt_t", [128, NCC], F32, kind="ExternalInput").ap()
    v.d_alog = nc.dram_tensor("a_log_t", [128, NCC * S], F32, kind="ExternalInput").ap()
    v.d_dsk = nc.dram_tensor("d_t", [128, NCC], F32, kind="ExternalInput").ap()
    v.d_wout = nc.dram_tensor("w_out_t", [NDM, 128, NCC * 128], F32, kind="ExternalInput").ap()
    v.d_w1 = nc.dram_tensor("w1_t", [NH, 128, NK * 128], F32, kind="ExternalInput").ap()
    v.d_w3 = nc.dram_tensor("w3_t", [NH, 128, NK * 128], F32, kind="ExternalInput").ap()
    v.d_w2 = nc.dram_tensor("w2_t", [NDM, 128, NH * 128], F32, kind="ExternalInput").ap()
    v.d_es = nc.dram_tensor("es_const", [32, 32 * 128], F32, kind="ExternalInput").ap()

    v.d_o = nc.dram_tensor("o", [DM, T], F32, kind="ExternalOutput").ap()
    v.d_xc32 = nc.dram_tensor("sc_xc32", [DI, T], F32, kind="Internal").ap()
    v.d_xc16 = nc.dram_tensor("sc_xc16", [DI, T], BF16, kind="Internal").ap()
    v.d_sz16 = nc.dram_tensor("sc_sz16", [DI, T], BF16, kind="Internal").ap()
    v.d_yg16 = nc.dram_tensor("sc_yg16", [DI, T], BF16, kind="Internal").ap()
    v.d_mix = nc.dram_tensor("sc_mix", [DM, T], F32, kind="Internal").ap()
    v.d_x1 = nc.dram_tensor("sc_x1", [DM, T], F32, kind="Internal").ap()
    v.d_ln2 = nc.dram_tensor("sc_ln2", [DM, T], BF16, kind="Internal").ap()
    v.d_g = nc.dram_tensor("sc_g", [HP, T], BF16, kind="Internal").ap()
    v.d_ffn = nc.dram_tensor("sc_ffn", [DM, T], F32, kind="Internal").ap()

    v.dbg = {}
    for name, shape in (("ln1", [DM, T]), ("xc", [DI, T]), ("sz", [DI, T]),
                        ("dtin", [128, T]), ("bc", [32, T]), ("yg", [DI, T]),
                        ("mix", [DM, T]), ("x1", [DM, T]), ("ln2", [DM, T]),
                        ("g", [HP, T]), ("ffn", [DM, T]), ("stats", [128, 16]),
                        ("w1q0", [128, 2048]), ("h1r0", [128, 512]),
                        ("ts0", [128, 512]), ("l2s0", [128, 2048])):
        if name in v.debug:
            v.dbg[name] = nc.dram_tensor("dbg_" + name, shape, F32,
                                         kind="ExternalOutput").ap()

    es = contextlib.ExitStack()
    with es:
        def sb(name, shape, dt_=F32):
            return es.enter_context(nc.sbuf_tensor(name, shape, dt_))

        v.ones16 = sb("ones16", [128, 128], BF16)
        v.onesf_col = sb("onesf_col", [128, 1], F32)
        v.onesf_row = sb("onesf_row", [1, 128], F32)
        v.st_acc = sb("st_acc", [128, 3 * N_WT])
        v.st_scal = sb("st_scal", [128, 9])
        v.epsc = sb("epsc", [128, 1])

        v.stats_plan = []
        for wi, dr in ((0, v.d_w1), (1, v.d_w3), (2, v.d_w2)):
            for ti in range(N_WT):
                v.stats_plan.append((wi, dr, ti))

        def stats_tile_src(j):
            wi, dr, ti = v.stats_plan[j]
            return dr.rearrange("a p n -> (a p n)")[
                ti * 128 * 2048:(ti + 1) * 128 * 2048].rearrange(
                "(p n) -> p n", p=128)
        v.stats_tile_src = stats_tile_src

        with contextlib.ExitStack() as mscope:
            def mb(name, shape, dt_=F32):
                return mscope.enter_context(nc.sbuf_tensor(name, shape, dt_))
            v.cw0, v.cw1, v.cw2 = mb("cw0", [128, NCC]), mb("cw1", [128, NCC]), mb("cw2", [128, NCC])
            v.cb = mb("cb", [128, NCC])
            v.bdt = mb("bdt", [128, NCC])
            v.dsk = mb("dsk", [128, NCC])
            v.asb = mb("asb", [128, NCC * S])
            v.wx16 = mb("wx16", [128, NCC, 160], BF16)
            v.wdt16 = mb("wdt16", [128, DI], BF16)
            v.es16 = mb("es16", [32, 32 * 128], BF16)
            v.carry = mb("carry", [128, NCC * S])
            v.dtinT = mb("dtinT", [128, T], BF16)
            v.bc32 = mb("bc32", [32, T])
            v.bc16 = mb("bc16", [32, T], BF16)
            _p0_consts(nc, v)
            with contextlib.ExitStack() as scope12:
                ln1 = scope12.enter_context(nc.sbuf_tensor("ln1", [128, NK, T], BF16))
                if upto >= 1:
                    _p1_ln1(nc, v, ln1)
                if upto >= 2:
                    _p2_xz(nc, v, ln1)
            if upto >= 3:
                _p3_xdbl(nc, v)
            if upto >= 4:
                _p4_scan(nc, v)
                if "yg" in v.dbg:
                    _dump_scratch_now(nc, v, "yg", v.d_yg16, BF16)
        if upto >= 5:
            _p5a_wout(nc, v)
            _p5_stats_fin(nc, v)
            _ln_residual_phase(nc, "p5b", v.d_mix, v.d_x, v.d_x1, v,
                               also_ln_out=v.d_ln2)
        if upto >= 6:
            _p6_ffna(nc, v)
        if upto >= 7:
            _p7a_ffnb(nc, v)
            _ln_residual_phase(nc, "p7b", v.d_ffn, v.d_x1, v.d_o, v)
        if v.dbg:
            _emit_debug(nc, v)
    return nc



def _p0_consts(nc, v):
    with contextlib.ExitStack() as ps:
        wxst = ps.enter_context(nc.sbuf_tensor("p0wx", [128, NCC * 160], F32))
        wdtst = ps.enter_context(nc.sbuf_tensor("p0wdt", [128, DI], F32))
        esst = ps.enter_context(nc.sbuf_tensor("p0es", [32, 32 * 128], F32))
        ph = LS("p0")
        ph.dma("sync", lambda e: e.dma_start(out=v.cw0[:], in_=v.d_cw[:, 0:NCC]))
        ph.dma("sync", lambda e: e.dma_start(out=v.cw1[:], in_=v.d_cw[:, NCC:2 * NCC]))
        ph.dma("sync", lambda e: e.dma_start(out=v.cw2[:], in_=v.d_cw[:, 2 * NCC:3 * NCC]))
        ph.dma("sync", lambda e: e.dma_start(out=v.cb[:], in_=v.d_cb))
        ph.dma("sync", lambda e: e.dma_start(out=v.bdt[:], in_=v.d_bdt))
        ph.dma("sync", lambda e: e.dma_start(out=v.dsk[:], in_=v.d_dsk))
        ph.dma("sync", lambda e: e.dma_start(out=v.asb[:], in_=v.d_alog))
        ph.dma("sync", lambda e: e.dma_start(
            out=wxst[:].rearrange("p (c f) -> p c f", f=160),
            in_=v.d_wx.rearrange("c p f -> p c f")))
        ph.dma("sync", lambda e: e.dma_start(out=wdtst[:], in_=v.d_wdt))
        ph.dma("sync", lambda e: e.dma_start(out=esst[:], in_=v.d_es))

        def p0_scalar(e):
            e.copy(out=v.wx16[:], in_=wxst[:].rearrange("p (c f) -> p c f", f=160))
            e.copy(out=v.wdt16[:], in_=wdtst[:])
            e.copy(out=v.es16[:], in_=esst[:])
            return e.activation(out=v.asb[:], in_=v.asb[:], func=AF.Exp)
        ph.on("scalar", p0_scalar)

        def p0_vector(e):
            e.tensor_scalar(out=v.asb[:], in0=v.asb[:], scalar1=-1.0, scalar2=None,
                            op0=OP.mult)
            e.memset(v.carry[:], 0.0)
            e.memset(v.epsc[:], EPS)
            e.memset(v.ones16[:], 1.0)
            e.memset(v.onesf_col[:], 1.0)
            e.memset(v.onesf_row[:], 1.0)
            return e.memset(v.st_acc[:], 0.0)
        ph.on("vector", p0_vector)
        ph.run(nc)


def _p1_ln1(nc, v, ln1):
    with contextlib.ExitStack() as ps:
        xsb = ps.enter_context(nc.sbuf_tensor("p1x", [128, 2, NK, TL], F32))
        xb16 = ps.enter_context(nc.sbuf_tensor("p1xb", [128, 2, NK * TL], BF16))
        xsq = ps.enter_context(nc.sbuf_tensor("p1xq", [128, 2, NK * TL], BF16))
        m_ = ps.enter_context(nc.sbuf_tensor("p1m", [128, TL], F32))
        q_ = ps.enter_context(nc.sbuf_tensor("p1q", [128, TL], F32))
        r_ = ps.enter_context(nc.sbuf_tensor("p1r", [128, TL], F32))
        mr = ps.enter_context(nc.sbuf_tensor("p1mr", [128, TL], F32))
        t_ = ps.enter_context(nc.sbuf_tensor("p1t", [128, NK, TL], F32))
        psu = ps.enter_context(nc.psum_tensor("p1ps", [128, 4 * TL], F32))
        xv = v.d_x.rearrange("(a p) t -> p a t", p=128)
        ph = Pipe("p1")
        R_x = [ph.res(), ph.res()]
        R_b = [ph.res(), ph.res()]
        R_ps = [ph.res(), ph.res()]
        R_st = ph.res()
        R_ln1 = ph.res()
        for i in range(NTL):
            bi = i % 2
            sl = slice(i * TL, (i + 1) * TL)
            pb = 2 * bi * TL
            ph.dma("sync", lambda e, sl=sl, bi=bi: e.dma_start(
                out=xsb[:, bi], in_=xv[:, :, sl]), w=[R_x[bi]])

            def cast(e, bi=bi):
                e.copy(out=xb16[:, bi], in_=xsb[:, bi].rearrange("p a t -> p (a t)"))
                return e.activation(out=xsq[:, bi], in_=xb16[:, bi], func=AF.Square)
            ph.step("scalar", cast, r=[R_x[bi]], w=[R_b[bi]])

            def mms(e, bi=bi, pb=pb):
                for k in range(NK):
                    e.matmul(psu[:, pb:pb + TL], v.ones16[:],
                             xb16[:, bi, k * TL:(k + 1) * TL],
                             start=(k == 0), stop=(k == NK - 1))
                last = None
                for k in range(NK):
                    last = e.matmul(psu[:, pb + TL:pb + 2 * TL], v.ones16[:],
                                    xsq[:, bi, k * TL:(k + 1) * TL],
                                    start=(k == 0), stop=(k == NK - 1))
                return last
            ph.step("tensor", mms, r=[R_b[bi]], w=[R_ps[bi]])

            def stats1(e, pb=pb):
                e.activation(out=m_[:], in_=psu[:, pb:pb + TL], func=AF.Copy,
                             scale=INV_DM)
                return e.activation(out=q_[:], in_=psu[:, pb + TL:pb + 2 * TL],
                                    func=AF.Copy, scale=INV_DM)
            ph.step("scalar", stats1, r=[R_ps[bi]], w=[R_st])

            def var(e):
                e.tensor_tensor(out=mr[:], in0=m_[:], in1=m_[:], op=OP.mult)
                return e.tensor_tensor(out=q_[:], in0=q_[:], in1=mr[:],
                                       op=OP.subtract)
            st_var = ph.step("vector", var, r=[R_st], w=[R_st])

            def rsq(e):
                e.activation(out=r_[:], in_=q_[:], func=AF.Ln, bias=v.epsc[:, 0:1])
                return e.activation(out=r_[:], in_=r_[:], func=AF.Exp, scale=-0.5)
            ph.step("scalar", rsq, r=[R_st], w=[R_st])

            def norm(e, sl=sl, bi=bi):
                e.tensor_tensor(out=mr[:], in0=m_[:], in1=r_[:], op=OP.mult)
                rb = r_[:].unsqueeze(1).broadcast_to((128, NK, TL))
                mb = mr[:].unsqueeze(1).broadcast_to((128, NK, TL))
                e.tensor_tensor(out=t_[:], in0=xsb[:, bi], in1=rb, op=OP.mult)
                return e.tensor_tensor(out=ln1[:, :, sl], in0=t_[:], in1=mb,
                                       op=OP.subtract)
            ph.step("vector", norm, r=[R_st, R_x[bi]], w=[R_ln1])
        if "ln1" in v.dbg:
            dbge = ps.enter_context(nc.sbuf_tensor("p1dbg", [128, T], F32))
            for k in range(NK):
                def cp(e, k=k):
                    return e.tensor_copy(out=dbge[:], in_=ln1[:, k, :])
                ph.step("vector", cp, r=[R_ln1], extra=ph.last_per_engine())
                ph.dma("sync", lambda e, k=k: e.dma_start(
                    out=v.dbg["ln1"][k * 128:(k + 1) * 128, :], in_=dbge[:]),
                    extra=[ph.steps[-1]])
        ph.run(nc)


def _p2_xz(nc, v, ln1):
    with contextlib.ExitStack() as ps:
        wraw = ps.enter_context(nc.sbuf_tensor("p2wr", [128, 2, NK * 128], F32))
        wb16 = ps.enter_context(nc.sbuf_tensor("p2wb", [128, 2, NK * 128], BF16))
        xprow = ps.enter_context(nc.sbuf_tensor("p2xp", [128, 2, T + 2], F32))
        v1 = ps.enter_context(nc.sbuf_tensor("p2v1", [128, T], F32))
        v3 = ps.enter_context(nc.sbuf_tensor("p2v3", [128, T], F32))
        xcrow = ps.enter_context(nc.sbuf_tensor("p2xc", [128, 2, T], F32))
        xc16r = ps.enter_context(nc.sbuf_tensor("p2xc16", [128, 2, T], BF16))
        szrow = ps.enter_context(nc.sbuf_tensor("p2sz", [128, 2, T], BF16))
        pp = ps.enter_context(nc.psum_tensor("p2ps", [128, 4 * T4], F32))
        ph = Pipe("p2")
        R_wr = [ph.res(), ph.res()]
        R_wb = [ph.res(), ph.res()]
        R_pp = [ph.res(), ph.res(), ph.res(), ph.res()]
        R_xp = [ph.res(), ph.res()]
        R_v3 = [ph.res()]
        R_xc = [ph.res(), ph.res()]
        R_sz = [ph.res(), ph.res()]
        def memz(e):
            e.memset(xprow[:, 0, 0:1], 0.0)
            e.memset(xprow[:, 0, T + 1:T + 2], 0.0)
            e.memset(xprow[:, 1, 0:1], 0.0)
            return e.memset(xprow[:, 1, T + 1:T + 2], 0.0)
        ph.step("vector", memz, w=[R_xp[0], R_xp[1]])

        def emit_w(n):
            bi = n % 2
            ph.dma("sync", lambda e, n=n, bi=bi: e.dma_start(
                out=wraw[:, bi], in_=v.d_win[n]), w=[R_wr[bi]])
            ph.step("scalar", lambda e, bi=bi: e.copy(out=wb16[:, bi],
                                                      in_=wraw[:, bi]),
                    r=[R_wr[bi]], w=[R_wb[bi]])

        emit_w(0)
        for n in range(64):
            bi = n % 2
            if n + 1 < 64:
                emit_w(n + 1)
            for t4 in range(NT4):
                sl = slice(t4 * T4, (t4 + 1) * T4)
                tp = t4 % 4
                bk = slice(tp * T4, (tp + 1) * T4)

                def mm(e, bi=bi, sl=sl, bk=bk):
                    last = None
                    for k in range(NK):
                        last = e.matmul(pp[:, bk],
                                        wb16[:, bi, k * 128:(k + 1) * 128],
                                        ln1[:, k, sl], start=(k == 0),
                                        stop=(k == NK - 1))
                    return last
                ph.step("tensor", mm, r=[R_wb[bi]], w=[R_pp[tp]])
                if n < 32:
                    ph.step("scalar", lambda e, t4=t4, bk=bk, bi=bi: e.copy(
                        out=xprow[:, bi, 1 + t4 * T4:1 + (t4 + 1) * T4],
                        in_=pp[:, bk]), r=[R_pp[tp]], w=[R_xp[bi]])
                else:
                    ph.step("scalar", lambda e, sl=sl, bk=bk, bi=bi: e.activation(
                        out=szrow[:, bi, sl], in_=pp[:, bk], func=AF.Silu),
                        r=[R_pp[tp]], w=[R_sz[bi]])
            if n < 32:
                cc = n

                def conv(e, cc=cc, bi=bi):
                    row = xprow[:, bi]
                    e.tensor_scalar(out=v1[:], in0=row[:, 1:T + 1],
                                    scalar1=v.cw1[:, cc:cc + 1], scalar2=None,
                                    op0=OP.mult)
                    e.scalar_tensor_tensor(out=v3[:], in0=row[:, 0:T],
                                           scalar=v.cw0[:, cc:cc + 1], in1=v1[:],
                                           op0=OP.mult, op1=OP.add)
                    return e.scalar_tensor_tensor(out=v3[:], in0=row[:, 2:T + 2],
                                                  scalar=v.cw2[:, cc:cc + 1],
                                                  in1=v3[:], op0=OP.mult,
                                                  op1=OP.add)
                ph.step("vector", conv, r=[R_xp[bi]], w=[R_v3[0]])

                def silu(e, cc=cc, bi=bi):
                    e.activation(out=xcrow[:, bi], in_=v3[:], func=AF.Silu,
                                 bias=v.cb[:, cc:cc + 1])
                    return e.copy(out=xc16r[:, bi], in_=xcrow[:, bi])
                ph.step("scalar", silu, r=[R_v3[0]], w=[R_xc[bi]])
                ph.dma("gpsimd", lambda e, cc=cc, bi=bi: e.dma_start(
                    out=v.d_xc32[cc * 128:(cc + 1) * 128, :], in_=xcrow[:, bi]),
                    r=[R_xc[bi]])
                ph.dma("gpsimd", lambda e, cc=cc, bi=bi: e.dma_start(
                    out=v.d_xc16[cc * 128:(cc + 1) * 128, :], in_=xc16r[:, bi]),
                    r=[R_xc[bi]])
            else:
                cc = n - 32
                ph.dma("gpsimd", lambda e, cc=cc, bi=bi: e.dma_start(
                    out=v.d_sz16[cc * 128:(cc + 1) * 128, :], in_=szrow[:, bi]),
                    r=[R_sz[bi]])
        ph.run(nc)


def _p3_xdbl(nc, v):
    with contextlib.ExitStack() as ps:
        xcb = ps.enter_context(nc.sbuf_tensor("p3xc", [128, 2, T], BF16))
        stbuf = ps.enter_context(nc.sbuf_tensor("p3stb", [128, 2, 2048], F32))
        stdmp = ps.enter_context(nc.sbuf_tensor("p3std", [128, 2048], BF16))
        pdt = ps.enter_context(nc.psum_tensor("p3pdt", [128, T], F32))
        pbc = ps.enter_context(nc.psum_tensor("p3pbc", [32, T], F32))
        ph = Pipe("p3")
        R_xcb = [ph.res(), ph.res()]
        R_ps = ph.res()
        R_stb = [ph.res(), ph.res()]
        sj = [0]
        for cc in range(NCC):
            bi = cc % 2
            ph.dma("sync", lambda e, cc=cc, bi=bi: e.dma_start(
                out=xcb[:, bi], in_=v.d_xc16[cc * 128:(cc + 1) * 128, :]),
                w=[R_xcb[bi]])

            def mm(e, cc=cc, bi=bi):
                last = None
                for t4 in range(NT4):
                    sl = slice(t4 * T4, (t4 + 1) * T4)
                    e.matmul(pdt[:, sl], v.wx16[:, cc, 0:128], xcb[:, bi, sl],
                             start=(cc == 0), stop=(cc == NCC - 1))
                    last = e.matmul(pbc[:, sl], v.wx16[:, cc, 128:160],
                                    xcb[:, bi, sl],
                                    start=(cc == 0), stop=(cc == NCC - 1))
                return last
            ph.step("tensor", mm, r=[R_xcb[bi]], w=[R_ps])
            while sj[0] < ((cc + 1) * 65) // NCC:
                j = sj[0]
                bj = sj[0] % 2
                ph.dma("gpsimd", lambda e, j=j, bj=bj: e.dma_start(
                    out=stbuf[:, bj], in_=v.stats_tile_src(j)), w=[R_stb[bj]])

                def acc(e, j=j, bj=bj):
                    wi, dr, ti = v.stats_plan[j]
                    col = wi * N_WT + ti
                    return e.activation(out=stdmp[:], in_=stbuf[:, bj], func=AF.Abs,
                                        accum_out=v.st_acc[:, col:col + 1])
                ph.step("scalar", acc, r=[R_stb[bj]])
                sj[0] += 1

        def evac(e):
            e.copy(out=v.dtinT[:], in_=pdt[:])
            e.copy(out=v.bc32[:], in_=pbc[:])
            return e.copy(out=v.bc16[:], in_=pbc[:])
        ph.step("scalar", evac, r=[R_ps])
        ph.run(nc)


def _p4_scan(nc, v):
    NA = 3
    with contextlib.ExitStack() as ps:
        brep = ps.enter_context(nc.sbuf_tensor("p4br", [128, 2, S * T4S], BF16))
        crep = ps.enter_context(nc.sbuf_tensor("p4cr", [128, 2, S * T4S], BF16))
        xcs = ps.enter_context(nc.sbuf_tensor("p4xc", [128, 2, T4S], F32))
        szs = ps.enter_context(nc.sbuf_tensor("p4szb", [128, 2, T4S], BF16))
        u1 = ps.enter_context(nc.sbuf_tensor("p4u1", [128, 2, T4S], F32))
        r1 = ps.enter_context(nc.sbuf_tensor("p4r1", [128, 2, T4S], F32))
        dtv = ps.enter_context(nc.sbuf_tensor("p4dtv", [128, 2, T4S], F32))
        wv16 = ps.enter_context(nc.sbuf_tensor("p4w", [128, 2, T4S], BF16))
        aring = ps.enter_context(nc.sbuf_tensor("p4a", [128, NA, T4S], F32))
        bbuf = ps.enter_context(nc.sbuf_tensor("p4b", [128, T4S], BF16))
        hbuf = ps.enter_context(nc.sbuf_tensor("p4h", [128, S * T4S], BF16))
        prod = ps.enter_context(nc.sbuf_tensor("p4p", [128, S * T4S], BF16))
        ytmp = ps.enter_context(nc.sbuf_tensor("p4yt", [128, 2 * T4S], F32))
        yv = ps.enter_context(nc.sbuf_tensor("p4y", [128, T4S], F32))
        ygv = ps.enter_context(nc.sbuf_tensor("p4yg", [128, 2, T4S], BF16))
        pdt4 = ps.enter_context(nc.psum_tensor("p4ps", [128, 4 * T4S], F32))
        pbr = ps.enter_context(nc.psum_tensor("p4pbr", [128, 2 * T4S], F32))
        ph = Pipe("p4")
        R_xcs = [ph.res(), ph.res()]
        R_szs = [ph.res(), ph.res()]
        R_pdt = [ph.res(), ph.res()]
        R_ur = [ph.res(), ph.res()]
        R_dtv = [ph.res(), ph.res()]
        R_wv = [ph.res(), ph.res()]
        R_a = [ph.res() for _ in range(NA)]
        R_yg = [ph.res(), ph.res()]
        R_br = [ph.res(), ph.res()]
        R_pbr = [ph.res(), ph.res()]
        R_carry = ph.res()
        R_scr = ph.res()

        NIT = NT4S * NCC

        def emit_brep(i4):
            pi = i4 % 2
            sl = slice(i4 * T4S, (i4 + 1) * T4S)
            for s in range(S):
                qi = s % 2
                b0 = qi * T4S

                def bmm(e, s=s, sl=sl, b0=b0):
                    return e.matmul(pbr[:, b0:b0 + T4S],
                                    v.es16[:, s * 128:(s + 1) * 128],
                                    v.bc16[:, sl], start=True, stop=True)
                ph.step("tensor", bmm, w=[R_pbr[qi]])

                def bev(e, s=s, pi=pi, b0=b0):
                    return e.copy(out=brep[:, pi, s * T4S:(s + 1) * T4S],
                                  in_=pbr[:, b0:b0 + T4S])
                ph.step("scalar", bev, r=[R_pbr[qi]], w=[R_br[pi]])
            for s in range(S):
                qi = s % 2
                b0 = qi * T4S

                def cmm(e, s=s, sl=sl, b0=b0):
                    return e.matmul(pbr[:, b0:b0 + T4S],
                                    v.es16[:, (16 + s) * 128:(17 + s) * 128],
                                    v.bc16[:, sl], start=True, stop=True)
                ph.step("tensor", cmm, w=[R_pbr[qi]])

                def cev(e, s=s, pi=pi, b0=b0):
                    return e.copy(out=crep[:, pi, s * T4S:(s + 1) * T4S],
                                  in_=pbr[:, b0:b0 + T4S])
                ph.step("scalar", cev, r=[R_pbr[qi]], w=[R_br[pi]])

        def emit_loads(it):
            i4, cc = divmod(it, NCC)
            bi = it % 2
            sl = slice(i4 * T4S, (i4 + 1) * T4S)
            ph.dma("sync", lambda e, cc=cc, bi=bi, sl=sl: e.dma_start(
                out=xcs[:, bi], in_=v.d_xc32[cc * 128:(cc + 1) * 128, sl]),
                w=[R_xcs[bi]])
            ph.dma("sync", lambda e, cc=cc, bi=bi, sl=sl: e.dma_start(
                out=szs[:, bi], in_=v.d_sz16[cc * 128:(cc + 1) * 128, sl]),
                w=[R_szs[bi]])

        def emit_prep(it):
            i4, cc = divmod(it, NCC)
            bi = it % 2
            sl = slice(i4 * T4S, (i4 + 1) * T4S)
            bk = slice(2 * bi * T4S, 2 * bi * T4S + T4S)
            ph.step("tensor", lambda e, cc=cc, bk=bk, sl=sl: e.matmul(
                pdt4[:, bk], v.wdt16[:, cc * 128:(cc + 1) * 128], v.dtinT[:, sl],
                start=True, stop=True), w=[R_pdt[bi]])

            def softplus(e, cc=cc, bi=bi, bk=bk):
                bcol = v.bdt[:, cc:cc + 1]
                e.activation(out=u1[:, bi], in_=pdt4[:, bk], func=AF.Abs, bias=bcol)
                e.activation(out=u1[:, bi], in_=u1[:, bi], func=AF.Exp, scale=-1.0)
                e.activation(out=u1[:, bi], in_=u1[:, bi], func=AF.Ln, bias=1.0)
                return e.activation(out=r1[:, bi], in_=pdt4[:, bk], func=AF.Relu,
                                    bias=bcol)
            ph.step("scalar", softplus, r=[R_pdt[bi]], w=[R_ur[bi]])

            def dtw(e, bi=bi):
                e.tensor_tensor(out=dtv[:, bi], in0=u1[:, bi], in1=r1[:, bi],
                                op=OP.add)
                return e.tensor_tensor(out=wv16[:, bi], in0=dtv[:, bi],
                                       in1=xcs[:, bi], op=OP.mult)
            ph.step("vector", dtw, r=[R_ur[bi], R_xcs[bi]], w=[R_dtv[bi], R_wv[bi]])

        def emit_aexp(it, s):
            i4, cc = divmod(it, NCC)
            bi = it % 2
            ai = s % NA

            def aexp(e, cc=cc, bi=bi, s=s, ai=ai):
                return e.activation(out=aring[:, ai], in_=dtv[:, bi], func=AF.Exp,
                                    scale=v.asb[:, cc * S + s:cc * S + s + 1])
            ph.step("scalar", aexp, r=[R_dtv[bi]], w=[R_a[ai]])

        def emit_scan_s(it, s):
            i4, cc = divmod(it, NCC)
            bi = it % 2
            pi = i4 % 2
            ai = s % NA
            ssl = slice(s * T4S, (s + 1) * T4S)

            def scan_s(e, cc=cc, bi=bi, pi=pi, s=s, ai=ai, ssl=ssl):
                e.tensor_tensor(out=bbuf[:], in0=wv16[:, bi],
                                in1=brep[:, pi, ssl], op=OP.mult)
                e.tensor_tensor_scan(
                    out=hbuf[:, ssl], data0=aring[:, ai], data1=bbuf[:],
                    initial=v.carry[:, cc * S + s:cc * S + s + 1],
                    op0=OP.mult, op1=OP.add)
                return e.tensor_tensor(out=prod[:, ssl], in0=hbuf[:, ssl],
                                       in1=crep[:, pi, ssl], op=OP.mult)
            ph.step("vector", scan_s,
                    r=[R_wv[bi], R_a[ai], R_br[pi], R_carry], w=[R_scr])

        def emit_tail(it):
            i4, cc = divmod(it, NCC)
            bi = it % 2
            sl = slice(i4 * T4S, (i4 + 1) * T4S)

            def tail(e, cc=cc, bi=bi):
                e.tensor_copy(
                    out=v.carry[:, cc * S:(cc + 1) * S],
                    in_=hbuf[:].rearrange("p (s t) -> p s t", s=S)
                    [:, :, T4S - 1:T4S].squeeze(2))
                e.tensor_tensor(out=prod[:, 0:8 * T4S], in0=prod[:, 0:8 * T4S],
                                in1=prod[:, 8 * T4S:16 * T4S], op=OP.add)
                e.tensor_tensor(out=prod[:, 0:4 * T4S], in0=prod[:, 0:4 * T4S],
                                in1=prod[:, 4 * T4S:8 * T4S], op=OP.add)
                e.tensor_tensor(out=ytmp[:], in0=prod[:, 0:2 * T4S],
                                in1=prod[:, 2 * T4S:4 * T4S], op=OP.add)
                e.tensor_tensor(out=yv[:], in0=ytmp[:, 0:T4S],
                                in1=ytmp[:, T4S:2 * T4S], op=OP.add)
                e.scalar_tensor_tensor(out=yv[:], in0=xcs[:, bi],
                                       scalar=v.dsk[:, cc:cc + 1], in1=yv[:],
                                       op0=OP.mult, op1=OP.add)
                return e.tensor_tensor(out=ygv[:, bi], in0=yv[:], in1=szs[:, bi],
                                       op=OP.mult)
            ph.step("vector", tail,
                    r=[R_scr, R_xcs[bi], R_szs[bi]],
                    w=[R_yg[bi], R_scr, R_carry])
            ph.dma("gpsimd", lambda e, cc=cc, bi=bi, sl=sl: e.dma_start(
                out=v.d_yg16[cc * 128:(cc + 1) * 128, sl], in_=ygv[:, bi]),
                r=[R_yg[bi]])

        for it in range(NIT):
            i4, cc = divmod(it, NCC)
            if cc == 0:
                emit_brep(i4)
            emit_loads(it)
            emit_prep(it)
            for s in range(S):
                emit_aexp(it, s)
                if s >= 1:
                    emit_scan_s(it, s - 1)
            emit_scan_s(it, S - 1)
            emit_tail(it)
        ph.run(nc)


def _p5_stats_fin(nc, v):
    with contextlib.ExitStack() as ps:
        stcol = ps.enter_context(nc.sbuf_tensor("pfc", [128, 3], F32))
        t_tot = ps.enter_context(nc.sbuf_tensor("pftt", [1, 3], F32))
        pstp = ps.enter_context(nc.psum_tensor("pfp", [128, 9], F32))
        ph = LS("pf")

        def red(e):
            last = None
            for wi in range(3):
                last = e.tensor_reduce(out=stcol[:, wi:wi + 1],
                                       in_=v.st_acc[:, wi * N_WT:(wi + 1) * N_WT],
                                       axis=AX.X, op=OP.add)
            return last
        ph.on("vector", red)
        ph.on("tensor", lambda e: e.matmul(pstp[0:1, 0:3], v.onesf_col[:],
                                           stcol[:], start=True, stop=True))
        ph.on("vector", lambda e: e.tensor_scalar(
            out=t_tot[:], in0=pstp[0:1, 0:3], scalar1=INV_W, scalar2=None,
            op0=OP.mult))
        ph.on("tensor", lambda e: e.matmul(pstp[:, 0:3], v.onesf_row[:], t_tot[:],
                                           start=True, stop=True))

        def fill(e):
            e.tensor_copy(out=v.st_scal[:, 0:3], in_=pstp[:, 0:3])
            e.tensor_scalar(out=v.st_scal[:, 3:6], in0=pstp[:, 0:3], scalar1=0.5,
                            scalar2=None, op0=OP.mult)
            return e.tensor_scalar(out=v.st_scal[:, 6:9], in0=pstp[:, 0:3],
                                   scalar1=-0.5, scalar2=None, op0=OP.mult)
        ph.on("vector", fill)
        ph.run(nc)


def _p5_stats_fin(nc, v):
    with contextlib.ExitStack() as ps:
        stcol = ps.enter_context(nc.sbuf_tensor("pfc", [128, 3], F32))
        t_tot = ps.enter_context(nc.sbuf_tensor("pftt", [1, 3], F32))
        pstp = ps.enter_context(nc.psum_tensor("pfp", [128, 9], F32))
        ph = LS("pf")

        def red(e):
            last = None
            for wi in range(3):
                last = e.tensor_reduce(out=stcol[:, wi:wi + 1],
                                       in_=v.st_acc[:, wi * N_WT:(wi + 1) * N_WT],
                                       axis=AX.X, op=OP.add)
            return last
        ph.on("vector", red)
        ph.on("tensor", lambda e: e.matmul(pstp[0:1, 0:3], v.onesf_col[:],
                                           stcol[:], start=True, stop=True))
        ph.on("vector", lambda e: e.tensor_scalar(
            out=t_tot[:], in0=pstp[0:1, 0:3], scalar1=INV_W, scalar2=None,
            op0=OP.mult))
        ph.on("tensor", lambda e: e.matmul(pstp[:, 0:3], v.onesf_row[:], t_tot[:],
                                           start=True, stop=True))

        def fill(e):
            e.tensor_copy(out=v.st_scal[:, 0:3], in_=pstp[:, 0:3])
            e.tensor_scalar(out=v.st_scal[:, 3:6], in0=pstp[:, 0:3], scalar1=0.5,
                            scalar2=None, op0=OP.mult)
            return e.tensor_scalar(out=v.st_scal[:, 6:9], in0=pstp[:, 0:3],
                                   scalar1=-0.5, scalar2=None, op0=OP.mult)
        ph.on("vector", fill)
        ph.run(nc)


def _p5_stats_fin(nc, v):
    with contextlib.ExitStack() as ps:
        stcol = ps.enter_context(nc.sbuf_tensor("pfc", [128, 3], F32))
        t_tot = ps.enter_context(nc.sbuf_tensor("pftt", [1, 3], F32))
        pstp = ps.enter_context(nc.psum_tensor("pfp", [128, 9], F32))
        ph = LS("pf")

        def red(e):
            last = None
            for wi in range(3):
                last = e.tensor_reduce(out=stcol[:, wi:wi + 1],
                                       in_=v.st_acc[:, wi * N_WT:(wi + 1) * N_WT],
                                       axis=AX.X, op=OP.add)
            return last
        ph.on("vector", red)
        ph.on("tensor", lambda e: e.matmul(pstp[0:1, 0:3], v.onesf_col[:],
                                           stcol[:], start=True, stop=True))
        ph.on("vector", lambda e: e.tensor_scalar(
            out=t_tot[:], in0=pstp[0:1, 0:3], scalar1=INV_W, scalar2=None,
            op0=OP.mult))
        ph.on("tensor", lambda e: e.matmul(pstp[:, 0:3], v.onesf_row[:], t_tot[:],
                                           start=True, stop=True))

        def fill(e):
            e.tensor_copy(out=v.st_scal[:, 0:3], in_=pstp[:, 0:3])
            e.tensor_scalar(out=v.st_scal[:, 3:6], in0=pstp[:, 0:3], scalar1=0.5,
                            scalar2=None, op0=OP.mult)
            return e.tensor_scalar(out=v.st_scal[:, 6:9], in0=pstp[:, 0:3],
                                   scalar1=-0.5, scalar2=None, op0=OP.mult)
        ph.on("vector", fill)
        ph.run(nc)


def _p5_stats_fin(nc, v):
    with contextlib.ExitStack() as ps:
        stcol = ps.enter_context(nc.sbuf_tensor("pfc", [128, 3], F32))
        t_tot = ps.enter_context(nc.sbuf_tensor("pftt", [1, 3], F32))
        pstp = ps.enter_context(nc.psum_tensor("pfp", [128, 9], F32))
        ph = LS("pf")

        def red(e):
            last = None
            for wi in range(3):
                last = e.tensor_reduce(out=stcol[:, wi:wi + 1],
                                       in_=v.st_acc[:, wi * N_WT:(wi + 1) * N_WT],
                                       axis=AX.X, op=OP.add)
            return last
        ph.on("vector", red)
        ph.on("tensor", lambda e: e.matmul(pstp[0:1, 0:3], v.onesf_col[:],
                                           stcol[:], start=True, stop=True))
        ph.on("vector", lambda e: e.tensor_scalar(
            out=t_tot[:], in0=pstp[0:1, 0:3], scalar1=INV_W, scalar2=None,
            op0=OP.mult))
        ph.on("tensor", lambda e: e.matmul(pstp[:, 0:3], v.onesf_row[:], t_tot[:],
                                           start=True, stop=True))

        def fill(e):
            e.tensor_copy(out=v.st_scal[:, 0:3], in_=pstp[:, 0:3])
            e.tensor_scalar(out=v.st_scal[:, 3:6], in0=pstp[:, 0:3], scalar1=0.5,
                            scalar2=None, op0=OP.mult)
            return e.tensor_scalar(out=v.st_scal[:, 6:9], in0=pstp[:, 0:3],
                                   scalar1=-0.5, scalar2=None, op0=OP.mult)
        ph.on("vector", fill)
        ph.run(nc)


def _p5a_wout(nc, v):
    with contextlib.ExitStack() as ps:
        TH = 1024
        ygh = ps.enter_context(nc.sbuf_tensor("p5yg", [128, NCC, TH], BF16))
        wraw = ps.enter_context(nc.sbuf_tensor("p5wr", [128, 2, NCC * 128], F32))
        wb16 = ps.enter_context(nc.sbuf_tensor("p5wb", [128, 2, NCC * 128], BF16))
        mrow = ps.enter_context(nc.sbuf_tensor("p5mr", [128, 2, TH], F32))
        stbuf = ps.enter_context(nc.sbuf_tensor("p5stb", [128, 2, 2048], F32))
        stdmp = ps.enter_context(nc.sbuf_tensor("p5std", [128, 2048], BF16))
        pmix = ps.enter_context(nc.psum_tensor("p5ps", [128, 2 * T4], F32))
        ygd = v.d_yg16.rearrange("(a p) t -> p a t", p=128)
        ph = Pipe("p5a")
        R_yg = ph.res()
        R_wr = [ph.res(), ph.res()]
        R_wb = [ph.res(), ph.res()]
        R_pm = [ph.res(), ph.res()]
        R_mr = [ph.res(), ph.res()]
        R_stb = [ph.res(), ph.res()]
        sj = [0]
        def emit_w(dm):
            bi = dm % 2
            ph.dma("sync", lambda e, dm=dm, bi=bi: e.dma_start(
                out=wraw[:, bi], in_=v.d_wout[dm]), w=[R_wr[bi]])
            ph.step("scalar", lambda e, bi=bi: e.copy(out=wb16[:, bi],
                                                      in_=wraw[:, bi]),
                    r=[R_wr[bi]], w=[R_wb[bi]])

        emit_w(0)
        for hf in range(2):
            th = slice(hf * TH, (hf + 1) * TH)
            ph.dma("sync", lambda e, th=th: e.dma_start(out=ygh[:],
                                                        in_=ygd[:, :, th]),
                   w=[R_yg])
            for dm in range(NDM):
                bi = dm % 2
                if hf * NDM + dm + 1 < 2 * NDM:
                    emit_w((dm + 1) % NDM)
                for t4 in range(2):
                    tp = t4 % 2
                    bk = slice(tp * T4, (tp + 1) * T4)

                    def mm(e, bi=bi, t4=t4, bk=bk):
                        last = None
                        for cc in range(NCC):
                            last = e.matmul(pmix[:, bk],
                                            wb16[:, bi, cc * 128:(cc + 1) * 128],
                                            ygh[:, cc, t4 * T4:(t4 + 1) * T4],
                                            start=(cc == 0), stop=(cc == NCC - 1))
                        return last
                    ph.step("tensor", mm, r=[R_wb[bi], R_yg], w=[R_pm[tp]])
                    ph.step("scalar", lambda e, t4=t4, bk=bk, bi=bi: e.copy(
                        out=mrow[:, bi, t4 * T4:(t4 + 1) * T4], in_=pmix[:, bk]),
                        r=[R_pm[tp]], w=[R_mr[bi]])
                ph.dma("gpsimd", lambda e, dm=dm, hf=hf, bi=bi: e.dma_start(
                    out=v.d_mix[dm * 128:(dm + 1) * 128, hf * TH:(hf + 1) * TH],
                    in_=mrow[:, bi]), r=[R_mr[bi]])
                k = hf * NDM + dm
                while sj[0] < ((k + 1) * 64) // (2 * NDM):
                    j = 65 + sj[0]
                    bj = sj[0] % 2
                    ph.dma("sync", lambda e, j=j, bj=bj: e.dma_start(
                        out=stbuf[:, bj], in_=v.stats_tile_src(j)), w=[R_stb[bj]])

                    def acc(e, j=j, bj=bj):
                        wi, dr, ti = v.stats_plan[j]
                        col = wi * N_WT + ti
                        return e.activation(out=stdmp[:], in_=stbuf[:, bj],
                                            func=AF.Abs,
                                            accum_out=v.st_acc[:, col:col + 1])
                    ph.step("scalar", acc, r=[R_stb[bj]])
                    sj[0] += 1
        ph.run(nc)


def _ln_residual_phase(nc, name, d_src, d_res, d_out, v, also_ln_out=None):
    with contextlib.ExitStack() as ps:
        ssb = ps.enter_context(nc.sbuf_tensor(name + "_s", [128, 2, NDM, TL], F32))
        rsb = (ps.enter_context(nc.sbuf_tensor(name + "_x", [128, 2, NDM, TL], F32))
               if d_res is not None else None)
        b16 = ps.enter_context(nc.sbuf_tensor(name + "_b", [128, 2, NDM * TL], BF16))
        sq16 = ps.enter_context(nc.sbuf_tensor(name + "_qq", [128, 2, NDM * TL], BF16))
        m_ = ps.enter_context(nc.sbuf_tensor(name + "_m", [128, TL], F32))
        q_ = ps.enter_context(nc.sbuf_tensor(name + "_v", [128, TL], F32))
        r_ = ps.enter_context(nc.sbuf_tensor(name + "_r", [128, TL], F32))
        mr = ps.enter_context(nc.sbuf_tensor(name + "_mr", [128, TL], F32))
        t_ = ps.enter_context(nc.sbuf_tensor(name + "_t", [128, NDM, TL], F32))
        u_ = ps.enter_context(nc.sbuf_tensor(name + "_u", [128, 2, NDM, TL], F32))
        l2 = (ps.enter_context(nc.sbuf_tensor(name + "_l2", [128, 2, NDM, TL], BF16))
              if also_ln_out is not None else None)
        psu = ps.enter_context(nc.psum_tensor(name + "_ps", [128, 4 * TL], F32))
        sv = d_src.rearrange("(a p) t -> p a t", p=128)
        rv = d_res.rearrange("(a p) t -> p a t", p=128) if d_res is not None else None
        ov = d_out.rearrange("(a p) t -> p a t", p=128)
        lv = (also_ln_out.rearrange("(a p) t -> p a t", p=128)
              if also_ln_out is not None else None)
        ph = Pipe(name)
        R_s = [ph.res(), ph.res()]
        R_x = [ph.res(), ph.res()]
        R_b = [ph.res(), ph.res()]
        R_ps = [ph.res(), ph.res()]
        R_st = ph.res()
        R_u = [ph.res(), ph.res()]
        R_l2 = [ph.res(), ph.res()]

        def cast_from(src_fn, bi):
            def cast(e):
                e.copy(out=b16[:, bi], in_=src_fn())
                return e.activation(out=sq16[:, bi], in_=b16[:, bi], func=AF.Square)
            return cast

        def mk_mms(bi, pb):
            def mms(e):
                for k in range(NDM):
                    e.matmul(psu[:, pb:pb + TL], v.ones16[:],
                             b16[:, bi, k * TL:(k + 1) * TL],
                             start=(k == 0), stop=(k == NDM - 1))
                last = None
                for k in range(NDM):
                    last = e.matmul(psu[:, pb + TL:pb + 2 * TL], v.ones16[:],
                                    sq16[:, bi, k * TL:(k + 1) * TL],
                                    start=(k == 0), stop=(k == NDM - 1))
                return last
            return mms

        def mk_stats1(pb):
            def stats1(e):
                e.activation(out=m_[:], in_=psu[:, pb:pb + TL], func=AF.Copy,
                             scale=INV_DM)
                return e.activation(out=q_[:], in_=psu[:, pb + TL:pb + 2 * TL],
                                    func=AF.Copy, scale=INV_DM)
            return stats1

        def var(e):
            e.tensor_tensor(out=mr[:], in0=m_[:], in1=m_[:], op=OP.mult)
            return e.tensor_tensor(out=q_[:], in0=q_[:], in1=mr[:], op=OP.subtract)

        def rsq(e):
            e.activation(out=r_[:], in_=q_[:], func=AF.Ln, bias=v.epsc[:, 0:1])
            return e.activation(out=r_[:], in_=r_[:], func=AF.Exp, scale=-0.5)

        for i in range(NTL):
            bi = i % 2
            pb = 2 * bi * TL
            sl = slice(i * TL, (i + 1) * TL)
            ph.dma("sync", lambda e, sl=sl, bi=bi: e.dma_start(
                out=ssb[:, bi], in_=sv[:, :, sl]), w=[R_s[bi]])
            if rv is not None:
                ph.dma("sync", lambda e, sl=sl, bi=bi: e.dma_start(
                    out=rsb[:, bi], in_=rv[:, :, sl]), w=[R_x[bi]])
            ph.step("scalar", cast_from(
                lambda bi=bi: ssb[:, bi].rearrange("p a t -> p (a t)"), bi),
                r=[R_s[bi]], w=[R_b[bi]])
            ph.step("tensor", mk_mms(bi, pb), r=[R_b[bi]], w=[R_ps[bi]])
            ph.step("scalar", mk_stats1(pb), r=[R_ps[bi]], w=[R_st])
            ph.step("vector", var, r=[R_st], w=[R_st])
            ph.step("scalar", rsq, r=[R_st], w=[R_st])

            def norm(e, bi=bi):
                e.tensor_tensor(out=mr[:], in0=m_[:], in1=r_[:], op=OP.mult)
                rb = r_[:].unsqueeze(1).broadcast_to((128, NDM, TL))
                mb = mr[:].unsqueeze(1).broadcast_to((128, NDM, TL))
                e.tensor_tensor(out=t_[:], in0=ssb[:, bi], in1=rb, op=OP.mult)
                e.tensor_tensor(out=t_[:], in0=t_[:], in1=mb, op=OP.subtract)
                if rsb is not None:
                    return e.tensor_tensor(out=u_[:, bi], in0=t_[:], in1=rsb[:, bi],
                                           op=OP.add)
                return e.tensor_copy(out=u_[:, bi], in_=t_[:])
            rlist = [R_st, R_s[bi]] + ([R_x[bi]] if rv is not None else [])
            ph.step("vector", norm, r=rlist, w=[R_u[bi]])
            ph.dma("gpsimd", lambda e, sl=sl, bi=bi: e.dma_start(
                out=ov[:, :, sl], in_=u_[:, bi]), r=[R_u[bi]])
            if also_ln_out is not None:
                ph.step("scalar", cast_from(
                    lambda bi=bi: u_[:, bi].rearrange("p a t -> p (a t)"), bi),
                    r=[R_u[bi]], w=[R_b[bi]])
                ph.step("tensor", mk_mms(bi, pb), r=[R_b[bi]], w=[R_ps[bi]])
                ph.step("scalar", mk_stats1(pb), r=[R_ps[bi]], w=[R_st])
                ph.step("vector", var, r=[R_st], w=[R_st])
                ph.step("scalar", rsq, r=[R_st], w=[R_st])

                def norm2(e, bi=bi):
                    e.tensor_tensor(out=mr[:], in0=m_[:], in1=r_[:], op=OP.mult)
                    rb = r_[:].unsqueeze(1).broadcast_to((128, NDM, TL))
                    mb = mr[:].unsqueeze(1).broadcast_to((128, NDM, TL))
                    e.tensor_tensor(out=t_[:], in0=u_[:, bi], in1=rb, op=OP.mult)
                    return e.tensor_tensor(out=l2[:, bi], in0=t_[:], in1=mb,
                                           op=OP.subtract)
                ph.step("vector", norm2, r=[R_st, R_u[bi]], w=[R_l2[bi]])
                ph.dma("gpsimd", lambda e, sl=sl, bi=bi: e.dma_start(
                    out=lv[:, :, sl], in_=l2[:, bi]), r=[R_l2[bi]])
        ph.run(nc)


def _p6_ffna(nc, v):
    C_S1, C_S3, C_SH1, C_SH3, C_NH1, C_NH3 = 0, 1, 3, 4, 6, 7
    with contextlib.ExitStack() as ps:
        ln2s = ps.enter_context(nc.sbuf_tensor("p6l2", [128, NDM, T], BF16))
        w1raw = ps.enter_context(nc.sbuf_tensor("p6w1", [128, 3, NK * 128], F32))
        w3raw = ps.enter_context(nc.sbuf_tensor("p6w3", [128, 3, NK * 128], F32))
        w1q = ps.enter_context(nc.sbuf_tensor("p6q1", [128, 2, NK * 128], BF16))
        w3q = ps.enter_context(nc.sbuf_tensor("p6q3", [128, 2, NK * 128], BF16))
        qa = ps.enter_context(nc.sbuf_tensor("p6qa", [128, NK * 128], BF16))
        qb = ps.enter_context(nc.sbuf_tensor("p6qb", [128, NK * 128], BF16))
        tsil = ps.enter_context(nc.sbuf_tensor("p6ts", [128, 2, T4], BF16))
        th3 = ps.enter_context(nc.sbuf_tensor("p6t3", [128, 2, T4], BF16))
        grow = ps.enter_context(nc.sbuf_tensor("p6g", [128, 2, T], BF16))
        ph1 = ps.enter_context(nc.psum_tensor("p6p1", [128, 2 * T4], F32))
        ph3 = ps.enter_context(nc.psum_tensor("p6p3", [128, 2 * T4], F32))
        l2v = v.d_ln2.rearrange("(a p) t -> p a t", p=128)
        ph = Pipe("p6")
        R_l2 = ph.res()
        R_w1 = [ph.res(), ph.res(), ph.res()]
        R_w3 = [ph.res(), ph.res(), ph.res()]
        R_q = [ph.res(), ph.res()]
        R_ph = [ph.res(), ph.res()]
        R_ts = [ph.res(), ph.res()]
        R_g = [ph.res(), ph.res()]
        ph.dma("sync", lambda e: e.dma_start(out=ln2s[:], in_=l2v), w=[R_l2])

        def emit_w(h):
            bi = h % 2
            ti = h % 3
            ph.dma("sync", lambda e, h=h, ti=ti: e.dma_start(out=w1raw[:, ti],
                                                             in_=v.d_w1[h]),
                   w=[R_w1[ti]])
            ph.dma("sync", lambda e, h=h, ti=ti: e.dma_start(out=w3raw[:, ti],
                                                             in_=v.d_w3[h]),
                   w=[R_w3[ti]])

            def quant1(e, bi=bi, ti=ti):
                e.tensor_scalar(out=qa[:], in0=w1raw[:, ti],
                                scalar1=v.st_scal[:, C_SH1:C_SH1 + 1],
                                scalar2=None, op0=OP.is_ge)
                e.tensor_scalar(out=qb[:], in0=w1raw[:, ti],
                                scalar1=v.st_scal[:, C_NH1:C_NH1 + 1],
                                scalar2=None, op0=OP.is_lt)
                return e.tensor_tensor(out=w1q[:, bi], in0=qa[:], in1=qb[:],
                                       op=OP.subtract)
            ph.step("vector", quant1, r=[R_w1[ti]], w=[R_q[bi]])

            def quant3(e, bi=bi, ti=ti):
                e.tensor_scalar(out=qa[:], in0=w3raw[:, ti],
                                scalar1=v.st_scal[:, C_SH3:C_SH3 + 1],
                                scalar2=None, op0=OP.is_ge)
                e.tensor_scalar(out=qb[:], in0=w3raw[:, ti],
                                scalar1=v.st_scal[:, C_NH3:C_NH3 + 1],
                                scalar2=None, op0=OP.is_lt)
                return e.tensor_tensor(out=w3q[:, bi], in0=qa[:], in1=qb[:],
                                       op=OP.subtract)
            ph.step("vector", quant3, r=[R_w3[ti]], w=[R_q[bi]])

        emit_w(0)
        for h in range(NH):
            bi = h % 2
            if h + 1 < NH:
                emit_w(h + 1)
            for t4 in range(NT4):
                sl = slice(t4 * T4, (t4 + 1) * T4)
                tp = t4 % 2
                bk = slice(tp * T4, (tp + 1) * T4)

                def mm(e, bi=bi, sl=sl, bk=bk):
                    last = None
                    for k in range(NK):
                        e.matmul(ph1[:, bk], w1q[:, bi, k * 128:(k + 1) * 128],
                                 ln2s[:, k, sl], start=(k == 0), stop=(k == NK - 1))
                        last = e.matmul(ph3[:, bk],
                                        w3q[:, bi, k * 128:(k + 1) * 128],
                                        ln2s[:, k, sl], start=(k == 0),
                                        stop=(k == NK - 1))
                    return last
                ph.step("tensor", mm, r=[R_q[bi], R_l2], w=[R_ph[tp]])

                def evac(e, bk=bk, tp=tp):
                    e.activation(out=tsil[:, tp], in_=ph1[:, bk], func=AF.Silu,
                                 scale=v.st_scal[:, C_S1:C_S1 + 1])
                    return e.activation(out=th3[:, tp], in_=ph3[:, bk], func=AF.Copy,
                                        scale=v.st_scal[:, C_S3:C_S3 + 1])
                ph.step("scalar", evac, r=[R_ph[tp]], w=[R_ts[tp]])
                ph.step("vector", lambda e, sl=sl, tp=tp, bi=bi: e.tensor_tensor(
                    out=grow[:, bi, sl], in0=tsil[:, tp], in1=th3[:, tp],
                    op=OP.mult), r=[R_ts[tp]], w=[R_g[bi]])
            ph.dma("gpsimd", lambda e, h=h, bi=bi: e.dma_start(
                out=v.d_g[h * 128:(h + 1) * 128, :], in_=grow[:, bi]), r=[R_g[bi]])
        ph.run(nc)


def _p7a_ffnb(nc, v):
    C_SH2, C_NH2 = 5, 8
    with contextlib.ExitStack() as ps:
        TH = 1024
        gh = ps.enter_context(nc.sbuf_tensor("p7g", [128, NH, TH], BF16))
        w2raw = ps.enter_context(nc.sbuf_tensor("p7w2", [128, 2, NH * 128], F32))
        w2q = ps.enter_context(nc.sbuf_tensor("p7q2", [128, 2, NH * 128], BF16))
        qa = ps.enter_context(nc.sbuf_tensor("p7qa", [128, NH * 128], BF16))
        qb = ps.enter_context(nc.sbuf_tensor("p7qb", [128, NH * 128], BF16))
        frow = ps.enter_context(nc.sbuf_tensor("p7f", [128, 2, TH], F32))
        pf = ps.enter_context(nc.psum_tensor("p7ps", [128, 2 * T4], F32))
        gv = v.d_g.rearrange("(a p) t -> p a t", p=128)
        ph = Pipe("p7a")
        R_g = ph.res()
        R_wr = [ph.res(), ph.res()]
        R_wq = [ph.res(), ph.res()]
        R_pf = [ph.res(), ph.res()]
        R_fr = [ph.res(), ph.res()]

        def emit_w(dm):
            bi = dm % 2
            ph.dma("sync", lambda e, dm=dm, bi=bi: e.dma_start(
                out=w2raw[:, bi], in_=v.d_w2[dm]), w=[R_wr[bi]])

            def quant(e, bi=bi):
                e.tensor_scalar(out=qa[:], in0=w2raw[:, bi],
                                scalar1=v.st_scal[:, C_SH2:C_SH2 + 1],
                                scalar2=None, op0=OP.is_ge)
                e.tensor_scalar(out=qb[:], in0=w2raw[:, bi],
                                scalar1=v.st_scal[:, C_NH2:C_NH2 + 1],
                                scalar2=None, op0=OP.is_lt)
                return e.tensor_tensor(out=w2q[:, bi], in0=qa[:], in1=qb[:],
                                       op=OP.subtract)
            ph.step("vector", quant, r=[R_wr[bi]], w=[R_wq[bi]])

        emit_w(0)
        for hf in range(2):
            th = slice(hf * TH, (hf + 1) * TH)
            ph.dma("sync", lambda e, th=th: e.dma_start(out=gh[:], in_=gv[:, :, th]),
                   w=[R_g])
            for dm in range(NDM):
                bi = dm % 2
                if hf * NDM + dm + 1 < 2 * NDM:
                    emit_w((dm + 1) % NDM)
                for t4 in range(2):
                    tp = t4 % 2
                    bk = slice(tp * T4, (tp + 1) * T4)

                    def mm(e, bi=bi, t4=t4, bk=bk):
                        last = None
                        for hh in range(NH):
                            last = e.matmul(pf[:, bk],
                                            w2q[:, bi, hh * 128:(hh + 1) * 128],
                                            gh[:, hh, t4 * T4:(t4 + 1) * T4],
                                            start=(hh == 0), stop=(hh == NH - 1))
                        return last
                    ph.step("tensor", mm, r=[R_wq[bi], R_g], w=[R_pf[tp]])
                    ph.step("scalar", lambda e, t4=t4, bk=bk, bi=bi: e.activation(
                        out=frow[:, bi, t4 * T4:(t4 + 1) * T4], in_=pf[:, bk],
                        func=AF.Copy, scale=v.st_scal[:, 2:3]),
                        r=[R_pf[tp]], w=[R_fr[bi]])
                ph.dma("gpsimd", lambda e, dm=dm, hf=hf, bi=bi: e.dma_start(
                    out=v.d_ffn[dm * 128:(dm + 1) * 128, hf * TH:(hf + 1) * TH],
                    in_=frow[:, bi]), r=[R_fr[bi]])
        ph.run(nc)


def _dump_scratch_now(nc, v, name, src_ap, dt_):
    ap = v.dbg.pop(name)
    with contextlib.ExitStack() as ps:
        buf32 = ps.enter_context(nc.sbuf_tensor("dnow32" + name, [128, T], F32))
        buf16 = ps.enter_context(nc.sbuf_tensor("dnow16" + name, [128, T], BF16))
        ph = LS("pd_" + name)
        rows = src_ap.shape[0]
        for r0 in range(0, rows, 128):
            if dt_ is BF16:
                ph.dma("sync", lambda e, r0=r0: e.dma_start(
                    out=buf16[:], in_=src_ap[r0:r0 + 128, :]))
                ph.on("vector", lambda e: e.tensor_copy(out=buf32[:], in_=buf16[:]))
            else:
                ph.dma("sync", lambda e, r0=r0: e.dma_start(
                    out=buf32[:], in_=src_ap[r0:r0 + 128, :]))
                ph.on("vector", lambda e: e.tensor_copy(out=buf32[:], in_=buf32[:]))
            ph.dma("sync", lambda e, r0=r0: e.dma_start(
                out=ap[r0:r0 + 128, :], in_=buf32[:]))
        ph.run(nc)


def _emit_debug(nc, v):
    srcs = {"xc": (v.d_xc32, F32), "sz": (v.d_sz16, BF16), "yg": (v.d_yg16, BF16),
            "mix": (v.d_mix, F32), "x1": (v.d_x1, F32), "ln2": (v.d_ln2, BF16),
            "g": (v.d_g, BF16), "ffn": (v.d_ffn, F32)}
    with contextlib.ExitStack() as ps:
        buf32 = ps.enter_context(nc.sbuf_tensor("dbg32", [128, T], F32))
        buf16 = ps.enter_context(nc.sbuf_tensor("dbg16", [128, T], BF16))
        ph = LS("pdbg")
        for name, ap in v.dbg.items():
            if name in ("ln1", "w1q0", "h1r0", "ts0", "l2s0"):
                continue
            if name == "dtin":
                ph.on("vector", lambda e: e.tensor_copy(out=buf32[:],
                                                        in_=v.dtinT[:]))
                ph.dma("sync", lambda e, ap=ap: e.dma_start(out=ap, in_=buf32[:]))
                continue
            if name == "bc":
                ph.on("vector", lambda e: e.tensor_copy(out=buf32[0:32, :],
                                                        in_=v.bc32[:]))
                ph.dma("sync", lambda e, ap=ap: e.dma_start(out=ap,
                                                            in_=buf32[0:32, :]))
                continue
            if name == "stats":
                def cp(e):
                    e.tensor_copy(out=buf32[:, 0:9], in_=v.st_scal[:])
                    return e.tensor_copy(out=buf32[:, 9:12], in_=v.st_acc[:, 0:3])
                ph.on("vector", cp)
                ph.dma("sync", lambda e, ap=ap: e.dma_start(out=ap,
                                                            in_=buf32[:, 0:16]))
                continue
            src, dt_ = srcs[name]
            rows = src.shape[0]
            for r0 in range(0, rows, 128):
                if dt_ is BF16:
                    ph.dma("sync", lambda e, src=src, r0=r0: e.dma_start(
                        out=buf16[:], in_=src[r0:r0 + 128, :]))
                    ph.on("vector", lambda e: e.tensor_copy(out=buf32[:],
                                                            in_=buf16[:]))
                else:
                    ph.dma("sync", lambda e, src=src, r0=r0: e.dma_start(
                        out=buf32[:], in_=src[r0:r0 + 128, :]))
                    ph.on("vector", lambda e: e.tensor_copy(out=buf32[:],
                                                            in_=buf32[:]))
                ph.dma("sync", lambda e, ap=ap, r0=r0: e.dma_start(
                    out=ap[r0:r0 + 128, :], in_=buf32[:]))
        ph.run(nc)



def _pretile(inp):
    f = np.ascontiguousarray
    W_in = inp["L0_attn_W_in"]
    w_in_t = f(W_in.reshape(NK, 128, 64, 128).transpose(2, 1, 0, 3)
               .reshape(64, 128, NK * 128))
    conv_w = inp["L0_attn_conv_w"][:, 0, :]
    conv_w_t = f(np.concatenate(
        [conv_w[:, j].reshape(NCC, 128).T for j in range(3)], axis=1))
    conv_b_t = f(inp["L0_attn_conv_b"].reshape(NCC, 128).T)
    w_x_t = f(inp["L0_attn_W_x"].reshape(NCC, 128, 160))
    w_dt_t = f(inp["L0_attn_W_dt"])
    b_dt_t = f(inp["L0_attn_b_dt"].reshape(NCC, 128).T)
    a_log_t = f(inp["L0_attn_A_log"].reshape(NCC, 128, S).transpose(1, 0, 2)
                .reshape(128, NCC * S))
    d_t = f(inp["L0_attn_D"].reshape(NCC, 128).T)
    w_out_t = f(inp["L0_attn_W_out"].reshape(NCC, 128, NDM, 128)
                .transpose(2, 1, 0, 3).reshape(NDM, 128, NCC * 128))
    w1 = np.zeros((HP, DM), np.float32); w1[:H_TRUE] = inp["L0_ffn_w1"]
    w3 = np.zeros((HP, DM), np.float32); w3[:H_TRUE] = inp["L0_ffn_w3"]
    w2 = np.zeros((DM, HP), np.float32); w2[:, :H_TRUE] = inp["L0_ffn_w2"]
    w1_t = f(w1.T.reshape(NK, 128, NH, 128).transpose(2, 1, 0, 3)
             .reshape(NH, 128, NK * 128))
    w3_t = f(w3.T.reshape(NK, 128, NH, 128).transpose(2, 1, 0, 3)
             .reshape(NH, 128, NK * 128))
    w2_t = f(w2.T.reshape(NH, 128, NDM, 128).transpose(2, 1, 0, 3)
             .reshape(NDM, 128, NH * 128))
    es_const = np.zeros((32, 32 * 128), np.float32)
    for s in range(32):
        es_const[s, s * 128:(s + 1) * 128] = 1.0
    return dict(w_in_t=w_in_t, conv_w_t=conv_w_t, conv_b_t=conv_b_t,
                w_x_t=w_x_t, w_dt_t=w_dt_t, b_dt_t=b_dt_t, a_log_t=a_log_t,
                d_t=d_t, w_out_t=w_out_t, w1_t=w1_t, w3_t=w3_t, w2_t=w2_t,
                es_const=es_const)


def make_in_maps(inputs):
    x = np.asarray(inputs["L0_attn_x"], dtype=np.float32)
    shared = _pretile({k: np.asarray(v_, dtype=np.float32)
                       for k, v_ in inputs.items() if k != "L0_attn_x"})
    in_maps = []
    for b in range(B):
        m = dict(shared)
        m["xT"] = np.ascontiguousarray(x[b].T)
        in_maps.append(m)
    return in_maps


def kernel(**inputs):
    in_maps = make_in_maps(inputs)
    nc = build()
    res = run_bass_kernel_spmd(nc, in_maps, core_ids=list(range(B)))
    out = np.empty((B, T, DM), np.float32)
    for b in range(B):
        out[b] = res.results[b]["o"].T
    return out


# version 09
# speedup vs baseline: 1.0919x; 1.0919x over previous
import contextlib
import numpy as np

import concourse.bass as bass
import concourse.mybir as mybir
from concourse.bass_utils import run_bass_kernel_spmd

F32 = mybir.dt.float32
BF16 = mybir.dt.bfloat16
AF = mybir.ActivationFunctionType
OP = mybir.AluOpType
AX = mybir.AxisListType

B, T, DM, DI, S, R = 8, 2048, 2048, 4096, 16, 128
H_TRUE = 5461
NH = 43
HP = NH * 128
NCC = DI // 128
NDM = DM // 128
NK = DM // 128
EPS = 1e-5
T4 = 512
NT4 = T // T4
TL = 256
NTL = T // TL
T4S = 512
NT4S = T // T4S
INV_DM = 1.0 / DM
INV_W = 1.0 / float(H_TRUE * DM)
N_WT = 43

DEFAULT_UPTO = 99


class LS:

    def __init__(self, name):
        self.name = name
        self.items = []

    def on(self, engine, fn, inc=1):
        self.items.append((engine, fn, inc))

    def dma(self, engine, fn):
        self.items.append((engine, fn, 16))

    def run(self, nc):
        total = sum(k for _, _, k in self.items)
        engines = []
        for e, _, _ in self.items:
            if e not in engines:
                engines.append(e)
        with contextlib.ExitStack() as ps:
            sem = ps.enter_context(nc.semaphore(self.name + "_g"))
            blk = ps.enter_context(nc.Block(self.name))
            for eng_name in engines:
                def make_body(eng_name):
                    def body(eng):
                        acc = 0
                        for (e, fn, inc) in self.items:
                            if e == eng_name:
                                if acc > 0:
                                    eng.wait_ge(sem, acc)
                                inst = fn(eng)
                                assert inst is not None, (
                                    f"{self.name}: step on {e} returned None")
                                inst.then_inc(sem, inc)
                            acc += inc
                        eng.wait_ge(sem, total)
                    return body
                getattr(blk, eng_name)(make_body(eng_name))


class PRes:
    __slots__ = ("name", "w", "r")

    def __init__(self, name):
        self.name = name
        self.w = None
        self.r = []


class PStep:
    __slots__ = ("engine", "fn", "inc", "deps", "gidx", "cum", "rres", "wres",
                 "semkey")

    def __init__(self, engine, fn, inc):
        self.engine = engine
        self.fn = fn
        self.inc = inc
        self.deps = set()
        self.gidx = 0
        self.cum = 0
        self.rres = ()
        self.wres = ()
        self.semkey = None


SERIAL_PHASES = set()


class Pipe:

    def __init__(self, name):
        self.name = name
        self.steps = []
        self.serial = name in SERIAL_PHASES

    def res(self, name="r"):
        return PRes(name)

    def step(self, engine, fn, r=(), w=(), extra=(), inc=1):
        s = PStep(engine, fn, inc)
        for rr in r:
            if rr.w is not None:
                s.deps.add(rr.w)
        for rr in w:
            for rd in rr.r:
                s.deps.add(rd)
            if rr.w is not None:
                s.deps.add(rr.w)
        for d in extra:
            if d is not None:
                s.deps.add(d)
        if self.serial and self.steps:
            s.deps.add(self.steps[-1])
        s.gidx = len(self.steps)
        s.rres = tuple(r)
        s.wres = tuple(w)
        self.steps.append(s)
        for rr in r:
            rr.r.append(s)
        for rr in w:
            rr.w = s
            rr.r = []
        return s

    def dma(self, engine, fn, r=(), w=(), extra=()):
        return self.step(engine, fn, r=r, w=w, extra=extra, inc=16)

    def last_per_engine(self):
        seen = {}
        for s in self.steps:
            seen[s.engine] = s
        return list(seen.values())

    def run(self, nc):
        engines = []
        for s in self.steps:
            if s.engine not in engines:
                engines.append(s.engine)
        for s in self.steps:
            if s.inc == 16:
                rs = (s.wres or s.rres)
                assert rs, f"{self.name}: dma step without resources"
                s.semkey = ("dma", id(rs[0]))
            else:
                s.semkey = ("eng", s.engine)
        cum = {}
        for s in self.steps:
            cum[s.semkey] = cum.get(s.semkey, 0) + s.inc
            s.cum = cum[s.semkey]
        totals = dict(cum)
        with contextlib.ExitStack() as ps:
            sems = {k: ps.enter_context(nc.semaphore(f"{self.name}_{i}"))
                    for i, k in enumerate(totals)}
            blk = ps.enter_context(nc.Block(self.name))
            for eng_name in engines:
                def make_body(eng_name):
                    def body(eng):
                        waited = {k: 0 for k in totals}
                        for s in self.steps:
                            if s.engine != eng_name:
                                continue
                            need = {}
                            for d in s.deps:
                                if d.semkey == ("eng", eng_name):
                                    continue
                                if d.engine == eng_name and d.semkey[0] == "eng":
                                    continue
                                need[d.semkey] = max(need.get(d.semkey, 0), d.cum)
                            for k, tgt in sorted(need.items(), key=str):
                                if tgt > waited[k]:
                                    eng.wait_ge(sems[k], tgt)
                                    waited[k] = tgt
                            inst = s.fn(eng)
                            assert inst is not None, (
                                f"{self.name}: step on {s.engine} returned None")
                            inst.then_inc(sems[s.semkey], s.inc)
                        for k in totals:
                            if k != ("eng", eng_name) and waited[k] < totals[k]:
                                eng.wait_ge(sems[k], totals[k])
                    return body
                getattr(blk, eng_name)(make_body(eng_name))


class Env:
    pass


def build(upto=DEFAULT_UPTO, debug=()):
    nc = bass.Bass("TRN2", target_bir_lowering=False,
                   detect_race_conditions=False)
    v = Env()
    v.debug = set(debug)

    v.d_x = nc.dram_tensor("xT", [DM, T], F32, kind="ExternalInput").ap()
    v.d_win = nc.dram_tensor("w_in_t", [64, 128, NK * 128], F32, kind="ExternalInput").ap()
    v.d_cw = nc.dram_tensor("conv_w_t", [128, NCC * 3], F32, kind="ExternalInput").ap()
    v.d_cb = nc.dram_tensor("conv_b_t", [128, NCC], F32, kind="ExternalInput").ap()
    v.d_wx = nc.dram_tensor("w_x_t", [NCC, 128, 160], F32, kind="ExternalInput").ap()
    v.d_wdt = nc.dram_tensor("w_dt_t", [128, DI], F32, kind="ExternalInput").ap()
    v.d_bdt = nc.dram_tensor("b_dt_t", [128, NCC], F32, kind="ExternalInput").ap()
    v.d_alog = nc.dram_tensor("a_log_t", [128, NCC * S], F32, kind="ExternalInput").ap()
    v.d_dsk = nc.dram_tensor("d_t", [128, NCC], F32, kind="ExternalInput").ap()
    v.d_wout = nc.dram_tensor("w_out_t", [NDM, 128, NCC * 128], F32, kind="ExternalInput").ap()
    v.d_w1 = nc.dram_tensor("w1_t", [NH, 128, NK * 128], F32, kind="ExternalInput").ap()
    v.d_w3 = nc.dram_tensor("w3_t", [NH, 128, NK * 128], F32, kind="ExternalInput").ap()
    v.d_w2 = nc.dram_tensor("w2_t", [NDM, 128, NH * 128], F32, kind="ExternalInput").ap()
    v.d_es = nc.dram_tensor("es_const", [32, 32 * 128], F32, kind="ExternalInput").ap()

    v.d_o = nc.dram_tensor("o", [DM, T], F32, kind="ExternalOutput").ap()
    v.d_xc32 = nc.dram_tensor("sc_xc32", [DI, T], F32, kind="Internal").ap()
    v.d_xc16 = nc.dram_tensor("sc_xc16", [DI, T], BF16, kind="Internal").ap()
    v.d_sz16 = nc.dram_tensor("sc_sz16", [DI, T], BF16, kind="Internal").ap()
    v.d_yg16 = nc.dram_tensor("sc_yg16", [DI, T], BF16, kind="Internal").ap()
    v.d_mix = nc.dram_tensor("sc_mix", [DM, T], F32, kind="Internal").ap()
    v.d_x1 = nc.dram_tensor("sc_x1", [DM, T], F32, kind="Internal").ap()
    v.d_ln2 = nc.dram_tensor("sc_ln2", [DM, T], BF16, kind="Internal").ap()
    v.d_g = nc.dram_tensor("sc_g", [HP, T], BF16, kind="Internal").ap()
    v.d_ffn = nc.dram_tensor("sc_ffn", [DM, T], F32, kind="Internal").ap()

    v.dbg = {}
    for name, shape in (("ln1", [DM, T]), ("xc", [DI, T]), ("sz", [DI, T]),
                        ("dtin", [128, T]), ("bc", [32, T]), ("yg", [DI, T]),
                        ("mix", [DM, T]), ("x1", [DM, T]), ("ln2", [DM, T]),
                        ("g", [HP, T]), ("ffn", [DM, T]), ("stats", [128, 16]),
                        ("w1q0", [128, 2048]), ("h1r0", [128, 512]),
                        ("ts0", [128, 512]), ("l2s0", [128, 2048])):
        if name in v.debug:
            v.dbg[name] = nc.dram_tensor("dbg_" + name, shape, F32,
                                         kind="ExternalOutput").ap()

    es = contextlib.ExitStack()
    with es:
        def sb(name, shape, dt_=F32):
            return es.enter_context(nc.sbuf_tensor(name, shape, dt_))

        v.ones16 = sb("ones16", [128, 128], BF16)
        v.onesf_col = sb("onesf_col", [128, 1], F32)
        v.onesf_row = sb("onesf_row", [1, 128], F32)
        v.st_acc = sb("st_acc", [128, 3 * N_WT])
        v.st_scal = sb("st_scal", [128, 9])
        v.epsc = sb("epsc", [128, 1])

        v.stats_plan = []
        for wi, dr in ((0, v.d_w1), (1, v.d_w3), (2, v.d_w2)):
            for ti in range(N_WT):
                v.stats_plan.append((wi, dr, ti))

        def stats_tile_src(j):
            wi, dr, ti = v.stats_plan[j]
            return dr.rearrange("a p n -> (a p n)")[
                ti * 128 * 2048:(ti + 1) * 128 * 2048].rearrange(
                "(p n) -> p n", p=128)
        v.stats_tile_src = stats_tile_src

        with contextlib.ExitStack() as mscope:
            def mb(name, shape, dt_=F32):
                return mscope.enter_context(nc.sbuf_tensor(name, shape, dt_))
            v.cw0, v.cw1, v.cw2 = mb("cw0", [128, NCC]), mb("cw1", [128, NCC]), mb("cw2", [128, NCC])
            v.cb = mb("cb", [128, NCC])
            v.bdt = mb("bdt", [128, NCC])
            v.dsk = mb("dsk", [128, NCC])
            v.asb = mb("asb", [128, NCC * S])
            v.wx16 = mb("wx16", [128, NCC, 160], BF16)
            v.wdt16 = mb("wdt16", [128, DI], BF16)
            v.es16 = mb("es16", [32, 32 * 128], BF16)
            v.carry = mb("carry", [128, NCC * S])
            v.dtinT = mb("dtinT", [128, T], BF16)
            v.bc32 = mb("bc32", [32, T])
            v.bc16 = mb("bc16", [32, T], BF16)
            _p0_consts(nc, v)
            with contextlib.ExitStack() as scope12:
                ln1 = scope12.enter_context(nc.sbuf_tensor("ln1", [128, NK, T], BF16))
                if upto >= 1:
                    _p1_ln1(nc, v, ln1)
                if upto >= 2:
                    _p2_xz(nc, v, ln1)
            if upto >= 3:
                _p3_xdbl(nc, v)
            if upto >= 4:
                _p4_scan(nc, v)
                if "yg" in v.dbg:
                    _dump_scratch_now(nc, v, "yg", v.d_yg16, BF16)
        if upto >= 5:
            _p5a_wout(nc, v)
            _p5_stats_fin(nc, v)
            _ln_residual_phase(nc, "p5b", v.d_mix, v.d_x, v.d_x1, v,
                               also_ln_out=v.d_ln2)
        if upto >= 6:
            _p6_ffna(nc, v)
        if upto >= 7:
            _p7a_ffnb(nc, v)
            _ln_residual_phase(nc, "p7b", v.d_ffn, v.d_x1, v.d_o, v)
        if v.dbg:
            _emit_debug(nc, v)
    return nc



def _p0_consts(nc, v):
    with contextlib.ExitStack() as ps:
        wxst = ps.enter_context(nc.sbuf_tensor("p0wx", [128, NCC * 160], F32))
        wdtst = ps.enter_context(nc.sbuf_tensor("p0wdt", [128, DI], F32))
        esst = ps.enter_context(nc.sbuf_tensor("p0es", [32, 32 * 128], F32))
        ph = LS("p0")
        ph.dma("sync", lambda e: e.dma_start(out=v.cw0[:], in_=v.d_cw[:, 0:NCC]))
        ph.dma("sync", lambda e: e.dma_start(out=v.cw1[:], in_=v.d_cw[:, NCC:2 * NCC]))
        ph.dma("sync", lambda e: e.dma_start(out=v.cw2[:], in_=v.d_cw[:, 2 * NCC:3 * NCC]))
        ph.dma("sync", lambda e: e.dma_start(out=v.cb[:], in_=v.d_cb))
        ph.dma("sync", lambda e: e.dma_start(out=v.bdt[:], in_=v.d_bdt))
        ph.dma("sync", lambda e: e.dma_start(out=v.dsk[:], in_=v.d_dsk))
        ph.dma("sync", lambda e: e.dma_start(out=v.asb[:], in_=v.d_alog))
        ph.dma("sync", lambda e: e.dma_start(
            out=wxst[:].rearrange("p (c f) -> p c f", f=160),
            in_=v.d_wx.rearrange("c p f -> p c f")))
        ph.dma("sync", lambda e: e.dma_start(out=wdtst[:], in_=v.d_wdt))
        ph.dma("sync", lambda e: e.dma_start(out=esst[:], in_=v.d_es))

        def p0_scalar(e):
            e.copy(out=v.wx16[:], in_=wxst[:].rearrange("p (c f) -> p c f", f=160))
            e.copy(out=v.wdt16[:], in_=wdtst[:])
            e.copy(out=v.es16[:], in_=esst[:])
            return e.activation(out=v.asb[:], in_=v.asb[:], func=AF.Exp)
        ph.on("scalar", p0_scalar)

        def p0_vector(e):
            e.tensor_scalar(out=v.asb[:], in0=v.asb[:], scalar1=-1.0, scalar2=None,
                            op0=OP.mult)
            e.memset(v.carry[:], 0.0)
            e.memset(v.epsc[:], EPS)
            e.memset(v.ones16[:], 1.0)
            e.memset(v.onesf_col[:], 1.0)
            e.memset(v.onesf_row[:], 1.0)
            return e.memset(v.st_acc[:], 0.0)
        ph.on("vector", p0_vector)
        ph.run(nc)


def _p1_ln1(nc, v, ln1):
    with contextlib.ExitStack() as ps:
        xsb = ps.enter_context(nc.sbuf_tensor("p1x", [128, 2, NK, TL], F32))
        xb16 = ps.enter_context(nc.sbuf_tensor("p1xb", [128, 2, NK * TL], BF16))
        xsq = ps.enter_context(nc.sbuf_tensor("p1xq", [128, 2, NK * TL], BF16))
        m_ = ps.enter_context(nc.sbuf_tensor("p1m", [128, TL], F32))
        q_ = ps.enter_context(nc.sbuf_tensor("p1q", [128, TL], F32))
        r_ = ps.enter_context(nc.sbuf_tensor("p1r", [128, TL], F32))
        mr = ps.enter_context(nc.sbuf_tensor("p1mr", [128, TL], F32))
        t_ = ps.enter_context(nc.sbuf_tensor("p1t", [128, NK, TL], F32))
        psu = ps.enter_context(nc.psum_tensor("p1ps", [128, 4 * TL], F32))
        xv = v.d_x.rearrange("(a p) t -> p a t", p=128)
        ph = Pipe("p1")
        R_x = [ph.res(), ph.res()]
        R_b = [ph.res(), ph.res()]
        R_ps = [ph.res(), ph.res()]
        R_st = ph.res()
        R_ln1 = ph.res()
        for i in range(NTL):
            bi = i % 2
            sl = slice(i * TL, (i + 1) * TL)
            pb = 2 * bi * TL
            ph.dma("sync", lambda e, sl=sl, bi=bi: e.dma_start(
                out=xsb[:, bi], in_=xv[:, :, sl]), w=[R_x[bi]])

            def cast(e, bi=bi):
                e.copy(out=xb16[:, bi], in_=xsb[:, bi].rearrange("p a t -> p (a t)"))
                return e.activation(out=xsq[:, bi], in_=xb16[:, bi], func=AF.Square)
            ph.step("scalar", cast, r=[R_x[bi]], w=[R_b[bi]])

            def mms(e, bi=bi, pb=pb):
                for k in range(NK):
                    e.matmul(psu[:, pb:pb + TL], v.ones16[:],
                             xb16[:, bi, k * TL:(k + 1) * TL],
                             start=(k == 0), stop=(k == NK - 1))
                last = None
                for k in range(NK):
                    last = e.matmul(psu[:, pb + TL:pb + 2 * TL], v.ones16[:],
                                    xsq[:, bi, k * TL:(k + 1) * TL],
                                    start=(k == 0), stop=(k == NK - 1))
                return last
            ph.step("tensor", mms, r=[R_b[bi]], w=[R_ps[bi]])

            def stats1(e, pb=pb):
                e.activation(out=m_[:], in_=psu[:, pb:pb + TL], func=AF.Copy,
                             scale=INV_DM)
                return e.activation(out=q_[:], in_=psu[:, pb + TL:pb + 2 * TL],
                                    func=AF.Copy, scale=INV_DM)
            ph.step("scalar", stats1, r=[R_ps[bi]], w=[R_st])

            def var(e):
                e.tensor_tensor(out=mr[:], in0=m_[:], in1=m_[:], op=OP.mult)
                return e.tensor_tensor(out=q_[:], in0=q_[:], in1=mr[:],
                                       op=OP.subtract)
            st_var = ph.step("vector", var, r=[R_st], w=[R_st])

            def rsq(e):
                e.activation(out=r_[:], in_=q_[:], func=AF.Ln, bias=v.epsc[:, 0:1])
                return e.activation(out=r_[:], in_=r_[:], func=AF.Exp, scale=-0.5)
            ph.step("scalar", rsq, r=[R_st], w=[R_st])

            def norm(e, sl=sl, bi=bi):
                e.tensor_tensor(out=mr[:], in0=m_[:], in1=r_[:], op=OP.mult)
                rb = r_[:].unsqueeze(1).broadcast_to((128, NK, TL))
                mb = mr[:].unsqueeze(1).broadcast_to((128, NK, TL))
                e.tensor_tensor(out=t_[:], in0=xsb[:, bi], in1=rb, op=OP.mult)
                return e.tensor_tensor(out=ln1[:, :, sl], in0=t_[:], in1=mb,
                                       op=OP.subtract)
            ph.step("vector", norm, r=[R_st, R_x[bi]], w=[R_ln1])
        if "ln1" in v.dbg:
            dbge = ps.enter_context(nc.sbuf_tensor("p1dbg", [128, T], F32))
            for k in range(NK):
                def cp(e, k=k):
                    return e.tensor_copy(out=dbge[:], in_=ln1[:, k, :])
                ph.step("vector", cp, r=[R_ln1], extra=ph.last_per_engine())
                ph.dma("sync", lambda e, k=k: e.dma_start(
                    out=v.dbg["ln1"][k * 128:(k + 1) * 128, :], in_=dbge[:]),
                    extra=[ph.steps[-1]])
        ph.run(nc)


def _p2_xz(nc, v, ln1):
    with contextlib.ExitStack() as ps:
        wraw = ps.enter_context(nc.sbuf_tensor("p2wr", [128, 2, NK * 128], F32))
        wb16 = ps.enter_context(nc.sbuf_tensor("p2wb", [128, 2, NK * 128], BF16))
        xprow = ps.enter_context(nc.sbuf_tensor("p2xp", [128, 2, T + 2], F32))
        v1 = ps.enter_context(nc.sbuf_tensor("p2v1", [128, T], F32))
        v3 = ps.enter_context(nc.sbuf_tensor("p2v3", [128, T], F32))
        xcrow = ps.enter_context(nc.sbuf_tensor("p2xc", [128, 2, T], F32))
        xc16r = ps.enter_context(nc.sbuf_tensor("p2xc16", [128, 2, T], BF16))
        szrow = ps.enter_context(nc.sbuf_tensor("p2sz", [128, 2, T], BF16))
        pp = ps.enter_context(nc.psum_tensor("p2ps", [128, 4 * T4], F32))
        ph = Pipe("p2")
        R_wr = [ph.res(), ph.res()]
        R_wb = [ph.res(), ph.res()]
        R_pp = [ph.res(), ph.res(), ph.res(), ph.res()]
        R_xp = [ph.res(), ph.res()]
        R_v3 = [ph.res()]
        R_xc = [ph.res(), ph.res()]
        R_sz = [ph.res(), ph.res()]
        def memz(e):
            e.memset(xprow[:, 0, 0:1], 0.0)
            e.memset(xprow[:, 0, T + 1:T + 2], 0.0)
            e.memset(xprow[:, 1, 0:1], 0.0)
            return e.memset(xprow[:, 1, T + 1:T + 2], 0.0)
        ph.step("vector", memz, w=[R_xp[0], R_xp[1]])

        def emit_w(n):
            bi = n % 2
            ph.dma("sync", lambda e, n=n, bi=bi: e.dma_start(
                out=wraw[:, bi], in_=v.d_win[n]), w=[R_wr[bi]])
            ph.step("scalar", lambda e, bi=bi: e.copy(out=wb16[:, bi],
                                                      in_=wraw[:, bi]),
                    r=[R_wr[bi]], w=[R_wb[bi]])

        emit_w(0)
        for n in range(64):
            bi = n % 2
            if n + 1 < 64:
                emit_w(n + 1)
            for t4 in range(NT4):
                sl = slice(t4 * T4, (t4 + 1) * T4)
                tp = t4 % 4
                bk = slice(tp * T4, (tp + 1) * T4)

                def mm(e, bi=bi, sl=sl, bk=bk):
                    last = None
                    for k in range(NK):
                        last = e.matmul(pp[:, bk],
                                        wb16[:, bi, k * 128:(k + 1) * 128],
                                        ln1[:, k, sl], start=(k == 0),
                                        stop=(k == NK - 1))
                    return last
                ph.step("tensor", mm, r=[R_wb[bi]], w=[R_pp[tp]])
                if n < 32:
                    ph.step("scalar", lambda e, t4=t4, bk=bk, bi=bi: e.copy(
                        out=xprow[:, bi, 1 + t4 * T4:1 + (t4 + 1) * T4],
                        in_=pp[:, bk]), r=[R_pp[tp]], w=[R_xp[bi]])
                else:
                    ph.step("scalar", lambda e, sl=sl, bk=bk, bi=bi: e.activation(
                        out=szrow[:, bi, sl], in_=pp[:, bk], func=AF.Silu),
                        r=[R_pp[tp]], w=[R_sz[bi]])
            if n < 32:
                cc = n

                def conv(e, cc=cc, bi=bi):
                    row = xprow[:, bi]
                    e.tensor_scalar(out=v1[:], in0=row[:, 1:T + 1],
                                    scalar1=v.cw1[:, cc:cc + 1], scalar2=None,
                                    op0=OP.mult)
                    e.scalar_tensor_tensor(out=v3[:], in0=row[:, 0:T],
                                           scalar=v.cw0[:, cc:cc + 1], in1=v1[:],
                                           op0=OP.mult, op1=OP.add)
                    return e.scalar_tensor_tensor(out=v3[:], in0=row[:, 2:T + 2],
                                                  scalar=v.cw2[:, cc:cc + 1],
                                                  in1=v3[:], op0=OP.mult,
                                                  op1=OP.add)
                ph.step("vector", conv, r=[R_xp[bi]], w=[R_v3[0]])

                def silu(e, cc=cc, bi=bi):
                    e.activation(out=xcrow[:, bi], in_=v3[:], func=AF.Silu,
                                 bias=v.cb[:, cc:cc + 1])
                    return e.copy(out=xc16r[:, bi], in_=xcrow[:, bi])
                ph.step("scalar", silu, r=[R_v3[0]], w=[R_xc[bi]])
                ph.dma("gpsimd", lambda e, cc=cc, bi=bi: e.dma_start(
                    out=v.d_xc32[cc * 128:(cc + 1) * 128, :], in_=xcrow[:, bi]),
                    r=[R_xc[bi]])
                ph.dma("gpsimd", lambda e, cc=cc, bi=bi: e.dma_start(
                    out=v.d_xc16[cc * 128:(cc + 1) * 128, :], in_=xc16r[:, bi]),
                    r=[R_xc[bi]])
            else:
                cc = n - 32
                ph.dma("gpsimd", lambda e, cc=cc, bi=bi: e.dma_start(
                    out=v.d_sz16[cc * 128:(cc + 1) * 128, :], in_=szrow[:, bi]),
                    r=[R_sz[bi]])
        ph.run(nc)


def _p3_xdbl(nc, v):
    with contextlib.ExitStack() as ps:
        xcb = ps.enter_context(nc.sbuf_tensor("p3xc", [128, 2, T], BF16))
        stbuf = ps.enter_context(nc.sbuf_tensor("p3stb", [128, 2, 2048], F32))
        stdmp = ps.enter_context(nc.sbuf_tensor("p3std", [128, 2048], BF16))
        pdt = ps.enter_context(nc.psum_tensor("p3pdt", [128, T], F32))
        pbc = ps.enter_context(nc.psum_tensor("p3pbc", [32, T], F32))
        ph = Pipe("p3")
        R_xcb = [ph.res(), ph.res()]
        R_ps = ph.res()
        R_stb = [ph.res(), ph.res()]
        sj = [0]
        for cc in range(NCC):
            bi = cc % 2
            ph.dma("sync", lambda e, cc=cc, bi=bi: e.dma_start(
                out=xcb[:, bi], in_=v.d_xc16[cc * 128:(cc + 1) * 128, :]),
                w=[R_xcb[bi]])

            def mm(e, cc=cc, bi=bi):
                last = None
                for t4 in range(NT4):
                    sl = slice(t4 * T4, (t4 + 1) * T4)
                    e.matmul(pdt[:, sl], v.wx16[:, cc, 0:128], xcb[:, bi, sl],
                             start=(cc == 0), stop=(cc == NCC - 1))
                    last = e.matmul(pbc[:, sl], v.wx16[:, cc, 128:160],
                                    xcb[:, bi, sl],
                                    start=(cc == 0), stop=(cc == NCC - 1))
                return last
            ph.step("tensor", mm, r=[R_xcb[bi]], w=[R_ps])
            while sj[0] < ((cc + 1) * 65) // NCC:
                j = sj[0]
                bj = sj[0] % 2
                ph.dma("gpsimd", lambda e, j=j, bj=bj: e.dma_start(
                    out=stbuf[:, bj], in_=v.stats_tile_src(j)), w=[R_stb[bj]])

                def acc(e, j=j, bj=bj):
                    wi, dr, ti = v.stats_plan[j]
                    col = wi * N_WT + ti
                    return e.activation(out=stdmp[:], in_=stbuf[:, bj], func=AF.Abs,
                                        accum_out=v.st_acc[:, col:col + 1])
                ph.step("scalar", acc, r=[R_stb[bj]])
                sj[0] += 1

        def evac(e):
            e.copy(out=v.dtinT[:], in_=pdt[:])
            e.copy(out=v.bc32[:], in_=pbc[:])
            return e.copy(out=v.bc16[:], in_=pbc[:])
        ph.step("scalar", evac, r=[R_ps])
        ph.run(nc)


def _p4_scan(nc, v):
    NA = 3
    with contextlib.ExitStack() as ps:
        brep = ps.enter_context(nc.sbuf_tensor("p4br", [128, 2, S * T4S], BF16))
        crep = ps.enter_context(nc.sbuf_tensor("p4cr", [128, 2, S * T4S], BF16))
        xcs = ps.enter_context(nc.sbuf_tensor("p4xc", [128, 2, T4S], F32))
        szs = ps.enter_context(nc.sbuf_tensor("p4szb", [128, 2, T4S], BF16))
        u1 = ps.enter_context(nc.sbuf_tensor("p4u1", [128, 2, T4S], F32))
        r1 = ps.enter_context(nc.sbuf_tensor("p4r1", [128, 2, T4S], F32))
        dtv = ps.enter_context(nc.sbuf_tensor("p4dtv", [128, 2, T4S], F32))
        wv16 = ps.enter_context(nc.sbuf_tensor("p4w", [128, 2, T4S], BF16))
        aring = ps.enter_context(nc.sbuf_tensor("p4a", [128, NA, T4S], F32))
        bbuf = ps.enter_context(nc.sbuf_tensor("p4b", [128, S * T4S], BF16))
        hbuf = ps.enter_context(nc.sbuf_tensor("p4h", [128, S * T4S], BF16))
        prod = ps.enter_context(nc.sbuf_tensor("p4p", [128, S * T4S], BF16))
        ytmp = ps.enter_context(nc.sbuf_tensor("p4yt", [128, 2 * T4S], F32))
        yv = ps.enter_context(nc.sbuf_tensor("p4y", [128, T4S], F32))
        ygv = ps.enter_context(nc.sbuf_tensor("p4yg", [128, 2, T4S], BF16))
        pdt4 = ps.enter_context(nc.psum_tensor("p4ps", [128, 4 * T4S], F32))
        pbr = ps.enter_context(nc.psum_tensor("p4pbr", [128, 2 * T4S], F32))
        ph = Pipe("p4")
        R_xcs = [ph.res(), ph.res()]
        R_szs = [ph.res(), ph.res()]
        R_pdt = [ph.res(), ph.res()]
        R_ur = [ph.res(), ph.res()]
        R_dtv = [ph.res(), ph.res()]
        R_wv = [ph.res(), ph.res()]
        R_a = [ph.res() for _ in range(NA)]
        R_yg = [ph.res(), ph.res()]
        R_br = [ph.res(), ph.res()]
        R_pbr = [ph.res(), ph.res()]
        R_carry = ph.res()
        R_scr = ph.res()

        NIT = NT4S * NCC

        def emit_brep(i4):
            pi = i4 % 2
            sl = slice(i4 * T4S, (i4 + 1) * T4S)
            for s in range(S):
                qi = s % 2
                b0 = qi * T4S

                def bmm(e, s=s, sl=sl, b0=b0):
                    return e.matmul(pbr[:, b0:b0 + T4S],
                                    v.es16[:, s * 128:(s + 1) * 128],
                                    v.bc16[:, sl], start=True, stop=True)
                ph.step("tensor", bmm, w=[R_pbr[qi]])

                def bev(e, s=s, pi=pi, b0=b0):
                    return e.copy(out=brep[:, pi, s * T4S:(s + 1) * T4S],
                                  in_=pbr[:, b0:b0 + T4S])
                ph.step("scalar", bev, r=[R_pbr[qi]], w=[R_br[pi]])
            for s in range(S):
                qi = s % 2
                b0 = qi * T4S

                def cmm(e, s=s, sl=sl, b0=b0):
                    return e.matmul(pbr[:, b0:b0 + T4S],
                                    v.es16[:, (16 + s) * 128:(17 + s) * 128],
                                    v.bc16[:, sl], start=True, stop=True)
                ph.step("tensor", cmm, w=[R_pbr[qi]])

                def cev(e, s=s, pi=pi, b0=b0):
                    return e.copy(out=crep[:, pi, s * T4S:(s + 1) * T4S],
                                  in_=pbr[:, b0:b0 + T4S])
                ph.step("scalar", cev, r=[R_pbr[qi]], w=[R_br[pi]])

        def emit_loads(it):
            i4, cc = divmod(it, NCC)
            bi = it % 2
            sl = slice(i4 * T4S, (i4 + 1) * T4S)
            ph.dma("sync", lambda e, cc=cc, bi=bi, sl=sl: e.dma_start(
                out=xcs[:, bi], in_=v.d_xc32[cc * 128:(cc + 1) * 128, sl]),
                w=[R_xcs[bi]])
            ph.dma("sync", lambda e, cc=cc, bi=bi, sl=sl: e.dma_start(
                out=szs[:, bi], in_=v.d_sz16[cc * 128:(cc + 1) * 128, sl]),
                w=[R_szs[bi]])

        def emit_prep(it):
            i4, cc = divmod(it, NCC)
            bi = it % 2
            sl = slice(i4 * T4S, (i4 + 1) * T4S)
            bk = slice(2 * bi * T4S, 2 * bi * T4S + T4S)
            ph.step("tensor", lambda e, cc=cc, bk=bk, sl=sl: e.matmul(
                pdt4[:, bk], v.wdt16[:, cc * 128:(cc + 1) * 128], v.dtinT[:, sl],
                start=True, stop=True), w=[R_pdt[bi]])

            def softplus(e, cc=cc, bi=bi, bk=bk):
                bcol = v.bdt[:, cc:cc + 1]
                e.activation(out=u1[:, bi], in_=pdt4[:, bk], func=AF.Abs, bias=bcol)
                e.activation(out=u1[:, bi], in_=u1[:, bi], func=AF.Exp, scale=-1.0)
                e.activation(out=u1[:, bi], in_=u1[:, bi], func=AF.Ln, bias=1.0)
                return e.activation(out=r1[:, bi], in_=pdt4[:, bk], func=AF.Relu,
                                    bias=bcol)
            ph.step("scalar", softplus, r=[R_pdt[bi]], w=[R_ur[bi]])

            def dtw(e, bi=bi):
                e.tensor_tensor(out=dtv[:, bi], in0=u1[:, bi], in1=r1[:, bi],
                                op=OP.add)
                return e.tensor_tensor(out=wv16[:, bi], in0=dtv[:, bi],
                                       in1=xcs[:, bi], op=OP.mult)
            ph.step("vector", dtw, r=[R_ur[bi], R_xcs[bi]], w=[R_dtv[bi], R_wv[bi]])

        def emit_ball(it):
            i4, cc = divmod(it, NCC)
            bi = it % 2
            pi = i4 % 2

            def ball(e, bi=bi, pi=pi):
                wb = wv16[:, bi].unsqueeze(1).broadcast_to((128, S, T4S))
                return e.tensor_tensor(
                    out=bbuf[:].rearrange("p (s t) -> p s t", s=S), in0=wb,
                    in1=brep[:, pi].rearrange("p (s t) -> p s t", s=S), op=OP.mult)
            ph.step("vector", ball, r=[R_wv[bi], R_br[pi]], w=[R_scr])

        def emit_aexp(it, s):
            i4, cc = divmod(it, NCC)
            bi = it % 2
            ai = s % NA

            def aexp(e, cc=cc, bi=bi, s=s, ai=ai):
                return e.activation(out=aring[:, ai], in_=dtv[:, bi], func=AF.Exp,
                                    scale=v.asb[:, cc * S + s:cc * S + s + 1])
            ph.step("scalar", aexp, r=[R_dtv[bi]], w=[R_a[ai]])

        def emit_scan_s(it, s):
            i4, cc = divmod(it, NCC)
            bi = it % 2
            pi = i4 % 2
            ai = s % NA
            ssl = slice(s * T4S, (s + 1) * T4S)

            def scan_s(e, cc=cc, bi=bi, pi=pi, s=s, ai=ai, ssl=ssl):
                return e.tensor_tensor_scan(
                    out=hbuf[:, ssl], data0=aring[:, ai], data1=bbuf[:, ssl],
                    initial=v.carry[:, cc * S + s:cc * S + s + 1],
                    op0=OP.mult, op1=OP.add)
            ph.step("vector", scan_s,
                    r=[R_a[ai], R_br[pi], R_carry, R_scr], w=[R_scr])

        def emit_tail(it):
            i4, cc = divmod(it, NCC)
            bi = it % 2
            sl = slice(i4 * T4S, (i4 + 1) * T4S)

            def tail(e, cc=cc, bi=bi, pi=i4 % 2):
                e.tensor_tensor(out=prod[:], in0=hbuf[:], in1=crep[:, pi],
                                op=OP.mult)
                e.tensor_copy(
                    out=v.carry[:, cc * S:(cc + 1) * S],
                    in_=hbuf[:].rearrange("p (s t) -> p s t", s=S)
                    [:, :, T4S - 1:T4S].squeeze(2))
                e.tensor_tensor(out=prod[:, 0:8 * T4S], in0=prod[:, 0:8 * T4S],
                                in1=prod[:, 8 * T4S:16 * T4S], op=OP.add)
                e.tensor_tensor(out=prod[:, 0:4 * T4S], in0=prod[:, 0:4 * T4S],
                                in1=prod[:, 4 * T4S:8 * T4S], op=OP.add)
                e.tensor_tensor(out=ytmp[:], in0=prod[:, 0:2 * T4S],
                                in1=prod[:, 2 * T4S:4 * T4S], op=OP.add)
                e.tensor_tensor(out=yv[:], in0=ytmp[:, 0:T4S],
                                in1=ytmp[:, T4S:2 * T4S], op=OP.add)
                e.scalar_tensor_tensor(out=yv[:], in0=xcs[:, bi],
                                       scalar=v.dsk[:, cc:cc + 1], in1=yv[:],
                                       op0=OP.mult, op1=OP.add)
                return e.tensor_tensor(out=ygv[:, bi], in0=yv[:], in1=szs[:, bi],
                                       op=OP.mult)
            ph.step("vector", tail,
                    r=[R_scr, R_xcs[bi], R_szs[bi]],
                    w=[R_yg[bi], R_scr, R_carry])
            ph.dma("gpsimd", lambda e, cc=cc, bi=bi, sl=sl: e.dma_start(
                out=v.d_yg16[cc * 128:(cc + 1) * 128, sl], in_=ygv[:, bi]),
                r=[R_yg[bi]])

        for it in range(NIT):
            i4, cc = divmod(it, NCC)
            if cc == 0:
                emit_brep(i4)
            emit_loads(it)
            emit_prep(it)
            emit_ball(it)
            for s in range(S):
                emit_aexp(it, s)
                if s >= 1:
                    emit_scan_s(it, s - 1)
            emit_scan_s(it, S - 1)
            emit_tail(it)
        ph.run(nc)


def _p5_stats_fin(nc, v):
    with contextlib.ExitStack() as ps:
        stcol = ps.enter_context(nc.sbuf_tensor("pfc", [128, 3], F32))
        t_tot = ps.enter_context(nc.sbuf_tensor("pftt", [1, 3], F32))
        pstp = ps.enter_context(nc.psum_tensor("pfp", [128, 9], F32))
        ph = LS("pf")

        def red(e):
            last = None
            for wi in range(3):
                last = e.tensor_reduce(out=stcol[:, wi:wi + 1],
                                       in_=v.st_acc[:, wi * N_WT:(wi + 1) * N_WT],
                                       axis=AX.X, op=OP.add)
            return last
        ph.on("vector", red)
        ph.on("tensor", lambda e: e.matmul(pstp[0:1, 0:3], v.onesf_col[:],
                                           stcol[:], start=True, stop=True))
        ph.on("vector", lambda e: e.tensor_scalar(
            out=t_tot[:], in0=pstp[0:1, 0:3], scalar1=INV_W, scalar2=None,
            op0=OP.mult))
        ph.on("tensor", lambda e: e.matmul(pstp[:, 0:3], v.onesf_row[:], t_tot[:],
                                           start=True, stop=True))

        def fill(e):
            e.tensor_copy(out=v.st_scal[:, 0:3], in_=pstp[:, 0:3])
            e.tensor_scalar(out=v.st_scal[:, 3:6], in0=pstp[:, 0:3], scalar1=0.5,
                            scalar2=None, op0=OP.mult)
            return e.tensor_scalar(out=v.st_scal[:, 6:9], in0=pstp[:, 0:3],
                                   scalar1=-0.5, scalar2=None, op0=OP.mult)
        ph.on("vector", fill)
        ph.run(nc)


def _p5_stats_fin(nc, v):
    with contextlib.ExitStack() as ps:
        stcol = ps.enter_context(nc.sbuf_tensor("pfc", [128, 3], F32))
        t_tot = ps.enter_context(nc.sbuf_tensor("pftt", [1, 3], F32))
        pstp = ps.enter_context(nc.psum_tensor("pfp", [128, 9], F32))
        ph = LS("pf")

        def red(e):
            last = None
            for wi in range(3):
                last = e.tensor_reduce(out=stcol[:, wi:wi + 1],
                                       in_=v.st_acc[:, wi * N_WT:(wi + 1) * N_WT],
                                       axis=AX.X, op=OP.add)
            return last
        ph.on("vector", red)
        ph.on("tensor", lambda e: e.matmul(pstp[0:1, 0:3], v.onesf_col[:],
                                           stcol[:], start=True, stop=True))
        ph.on("vector", lambda e: e.tensor_scalar(
            out=t_tot[:], in0=pstp[0:1, 0:3], scalar1=INV_W, scalar2=None,
            op0=OP.mult))
        ph.on("tensor", lambda e: e.matmul(pstp[:, 0:3], v.onesf_row[:], t_tot[:],
                                           start=True, stop=True))

        def fill(e):
            e.tensor_copy(out=v.st_scal[:, 0:3], in_=pstp[:, 0:3])
            e.tensor_scalar(out=v.st_scal[:, 3:6], in0=pstp[:, 0:3], scalar1=0.5,
                            scalar2=None, op0=OP.mult)
            return e.tensor_scalar(out=v.st_scal[:, 6:9], in0=pstp[:, 0:3],
                                   scalar1=-0.5, scalar2=None, op0=OP.mult)
        ph.on("vector", fill)
        ph.run(nc)


def _p5_stats_fin(nc, v):
    with contextlib.ExitStack() as ps:
        stcol = ps.enter_context(nc.sbuf_tensor("pfc", [128, 3], F32))
        t_tot = ps.enter_context(nc.sbuf_tensor("pftt", [1, 3], F32))
        pstp = ps.enter_context(nc.psum_tensor("pfp", [128, 9], F32))
        ph = LS("pf")

        def red(e):
            last = None
            for wi in range(3):
                last = e.tensor_reduce(out=stcol[:, wi:wi + 1],
                                       in_=v.st_acc[:, wi * N_WT:(wi + 1) * N_WT],
                                       axis=AX.X, op=OP.add)
            return last
        ph.on("vector", red)
        ph.on("tensor", lambda e: e.matmul(pstp[0:1, 0:3], v.onesf_col[:],
                                           stcol[:], start=True, stop=True))
        ph.on("vector", lambda e: e.tensor_scalar(
            out=t_tot[:], in0=pstp[0:1, 0:3], scalar1=INV_W, scalar2=None,
            op0=OP.mult))
        ph.on("tensor", lambda e: e.matmul(pstp[:, 0:3], v.onesf_row[:], t_tot[:],
                                           start=True, stop=True))

        def fill(e):
            e.tensor_copy(out=v.st_scal[:, 0:3], in_=pstp[:, 0:3])
            e.tensor_scalar(out=v.st_scal[:, 3:6], in0=pstp[:, 0:3], scalar1=0.5,
                            scalar2=None, op0=OP.mult)
            return e.tensor_scalar(out=v.st_scal[:, 6:9], in0=pstp[:, 0:3],
                                   scalar1=-0.5, scalar2=None, op0=OP.mult)
        ph.on("vector", fill)
        ph.run(nc)


def _p5_stats_fin(nc, v):
    with contextlib.ExitStack() as ps:
        stcol = ps.enter_context(nc.sbuf_tensor("pfc", [128, 3], F32))
        t_tot = ps.enter_context(nc.sbuf_tensor("pftt", [1, 3], F32))
        pstp = ps.enter_context(nc.psum_tensor("pfp", [128, 9], F32))
        ph = LS("pf")

        def red(e):
            last = None
            for wi in range(3):
                last = e.tensor_reduce(out=stcol[:, wi:wi + 1],
                                       in_=v.st_acc[:, wi * N_WT:(wi + 1) * N_WT],
                                       axis=AX.X, op=OP.add)
            return last
        ph.on("vector", red)
        ph.on("tensor", lambda e: e.matmul(pstp[0:1, 0:3], v.onesf_col[:],
                                           stcol[:], start=True, stop=True))
        ph.on("vector", lambda e: e.tensor_scalar(
            out=t_tot[:], in0=pstp[0:1, 0:3], scalar1=INV_W, scalar2=None,
            op0=OP.mult))
        ph.on("tensor", lambda e: e.matmul(pstp[:, 0:3], v.onesf_row[:], t_tot[:],
                                           start=True, stop=True))

        def fill(e):
            e.tensor_copy(out=v.st_scal[:, 0:3], in_=pstp[:, 0:3])
            e.tensor_scalar(out=v.st_scal[:, 3:6], in0=pstp[:, 0:3], scalar1=0.5,
                            scalar2=None, op0=OP.mult)
            return e.tensor_scalar(out=v.st_scal[:, 6:9], in0=pstp[:, 0:3],
                                   scalar1=-0.5, scalar2=None, op0=OP.mult)
        ph.on("vector", fill)
        ph.run(nc)


def _p5a_wout(nc, v):
    with contextlib.ExitStack() as ps:
        TH = 1024
        ygh = ps.enter_context(nc.sbuf_tensor("p5yg", [128, NCC, TH], BF16))
        wraw = ps.enter_context(nc.sbuf_tensor("p5wr", [128, 2, NCC * 128], F32))
        wb16 = ps.enter_context(nc.sbuf_tensor("p5wb", [128, 2, NCC * 128], BF16))
        mrow = ps.enter_context(nc.sbuf_tensor("p5mr", [128, 2, TH], F32))
        stbuf = ps.enter_context(nc.sbuf_tensor("p5stb", [128, 2, 2048], F32))
        stdmp = ps.enter_context(nc.sbuf_tensor("p5std", [128, 2048], BF16))
        pmix = ps.enter_context(nc.psum_tensor("p5ps", [128, 2 * T4], F32))
        ygd = v.d_yg16.rearrange("(a p) t -> p a t", p=128)
        ph = Pipe("p5a")
        R_yg = ph.res()
        R_wr = [ph.res(), ph.res()]
        R_wb = [ph.res(), ph.res()]
        R_pm = [ph.res(), ph.res()]
        R_mr = [ph.res(), ph.res()]
        R_stb = [ph.res(), ph.res()]
        sj = [0]
        def emit_w(dm):
            bi = dm % 2
            ph.dma("sync", lambda e, dm=dm, bi=bi: e.dma_start(
                out=wraw[:, bi], in_=v.d_wout[dm]), w=[R_wr[bi]])
            ph.step("scalar", lambda e, bi=bi: e.copy(out=wb16[:, bi],
                                                      in_=wraw[:, bi]),
                    r=[R_wr[bi]], w=[R_wb[bi]])

        emit_w(0)
        for hf in range(2):
            th = slice(hf * TH, (hf + 1) * TH)
            ph.dma("sync", lambda e, th=th: e.dma_start(out=ygh[:],
                                                        in_=ygd[:, :, th]),
                   w=[R_yg])
            for dm in range(NDM):
                bi = dm % 2
                if hf * NDM + dm + 1 < 2 * NDM:
                    emit_w((dm + 1) % NDM)
                for t4 in range(2):
                    tp = t4 % 2
                    bk = slice(tp * T4, (tp + 1) * T4)

                    def mm(e, bi=bi, t4=t4, bk=bk):
                        last = None
                        for cc in range(NCC):
                            last = e.matmul(pmix[:, bk],
                                            wb16[:, bi, cc * 128:(cc + 1) * 128],
                                            ygh[:, cc, t4 * T4:(t4 + 1) * T4],
                                            start=(cc == 0), stop=(cc == NCC - 1))
                        return last
                    ph.step("tensor", mm, r=[R_wb[bi], R_yg], w=[R_pm[tp]])
                    ph.step("scalar", lambda e, t4=t4, bk=bk, bi=bi: e.copy(
                        out=mrow[:, bi, t4 * T4:(t4 + 1) * T4], in_=pmix[:, bk]),
                        r=[R_pm[tp]], w=[R_mr[bi]])
                ph.dma("gpsimd", lambda e, dm=dm, hf=hf, bi=bi: e.dma_start(
                    out=v.d_mix[dm * 128:(dm + 1) * 128, hf * TH:(hf + 1) * TH],
                    in_=mrow[:, bi]), r=[R_mr[bi]])
                k = hf * NDM + dm
                while sj[0] < ((k + 1) * 64) // (2 * NDM):
                    j = 65 + sj[0]
                    bj = sj[0] % 2
                    ph.dma("sync", lambda e, j=j, bj=bj: e.dma_start(
                        out=stbuf[:, bj], in_=v.stats_tile_src(j)), w=[R_stb[bj]])

                    def acc(e, j=j, bj=bj):
                        wi, dr, ti = v.stats_plan[j]
                        col = wi * N_WT + ti
                        return e.activation(out=stdmp[:], in_=stbuf[:, bj],
                                            func=AF.Abs,
                                            accum_out=v.st_acc[:, col:col + 1])
                    ph.step("scalar", acc, r=[R_stb[bj]])
                    sj[0] += 1
        ph.run(nc)


def _ln_residual_phase(nc, name, d_src, d_res, d_out, v, also_ln_out=None):
    with contextlib.ExitStack() as ps:
        ssb = ps.enter_context(nc.sbuf_tensor(name + "_s", [128, 2, NDM, TL], F32))
        rsb = (ps.enter_context(nc.sbuf_tensor(name + "_x", [128, 2, NDM, TL], F32))
               if d_res is not None else None)
        b16 = ps.enter_context(nc.sbuf_tensor(name + "_b", [128, 2, NDM * TL], BF16))
        sq16 = ps.enter_context(nc.sbuf_tensor(name + "_qq", [128, 2, NDM * TL], BF16))
        m_ = ps.enter_context(nc.sbuf_tensor(name + "_m", [128, TL], F32))
        q_ = ps.enter_context(nc.sbuf_tensor(name + "_v", [128, TL], F32))
        r_ = ps.enter_context(nc.sbuf_tensor(name + "_r", [128, TL], F32))
        mr = ps.enter_context(nc.sbuf_tensor(name + "_mr", [128, TL], F32))
        t_ = ps.enter_context(nc.sbuf_tensor(name + "_t", [128, NDM, TL], F32))
        u_ = ps.enter_context(nc.sbuf_tensor(name + "_u", [128, 2, NDM, TL], F32))
        l2 = (ps.enter_context(nc.sbuf_tensor(name + "_l2", [128, 2, NDM, TL], BF16))
              if also_ln_out is not None else None)
        psu = ps.enter_context(nc.psum_tensor(name + "_ps", [128, 4 * TL], F32))
        sv = d_src.rearrange("(a p) t -> p a t", p=128)
        rv = d_res.rearrange("(a p) t -> p a t", p=128) if d_res is not None else None
        ov = d_out.rearrange("(a p) t -> p a t", p=128)
        lv = (also_ln_out.rearrange("(a p) t -> p a t", p=128)
              if also_ln_out is not None else None)
        ph = Pipe(name)
        R_s = [ph.res(), ph.res()]
        R_x = [ph.res(), ph.res()]
        R_b = [ph.res(), ph.res()]
        R_ps = [ph.res(), ph.res()]
        R_st = ph.res()
        R_u = [ph.res(), ph.res()]
        R_l2 = [ph.res(), ph.res()]

        def cast_from(src_fn, bi):
            def cast(e):
                e.copy(out=b16[:, bi], in_=src_fn())
                return e.activation(out=sq16[:, bi], in_=b16[:, bi], func=AF.Square)
            return cast

        def mk_mms(bi, pb):
            def mms(e):
                for k in range(NDM):
                    e.matmul(psu[:, pb:pb + TL], v.ones16[:],
                             b16[:, bi, k * TL:(k + 1) * TL],
                             start=(k == 0), stop=(k == NDM - 1))
                last = None
                for k in range(NDM):
                    last = e.matmul(psu[:, pb + TL:pb + 2 * TL], v.ones16[:],
                                    sq16[:, bi, k * TL:(k + 1) * TL],
                                    start=(k == 0), stop=(k == NDM - 1))
                return last
            return mms

        def mk_stats1(pb):
            def stats1(e):
                e.activation(out=m_[:], in_=psu[:, pb:pb + TL], func=AF.Copy,
                             scale=INV_DM)
                return e.activation(out=q_[:], in_=psu[:, pb + TL:pb + 2 * TL],
                                    func=AF.Copy, scale=INV_DM)
            return stats1

        def var(e):
            e.tensor_tensor(out=mr[:], in0=m_[:], in1=m_[:], op=OP.mult)
            return e.tensor_tensor(out=q_[:], in0=q_[:], in1=mr[:], op=OP.subtract)

        def rsq(e):
            e.activation(out=r_[:], in_=q_[:], func=AF.Ln, bias=v.epsc[:, 0:1])
            return e.activation(out=r_[:], in_=r_[:], func=AF.Exp, scale=-0.5)

        for i in range(NTL):
            bi = i % 2
            pb = 2 * bi * TL
            sl = slice(i * TL, (i + 1) * TL)
            ph.dma("sync", lambda e, sl=sl, bi=bi: e.dma_start(
                out=ssb[:, bi], in_=sv[:, :, sl]), w=[R_s[bi]])
            if rv is not None:
                ph.dma("sync", lambda e, sl=sl, bi=bi: e.dma_start(
                    out=rsb[:, bi], in_=rv[:, :, sl]), w=[R_x[bi]])
            ph.step("scalar", cast_from(
                lambda bi=bi: ssb[:, bi].rearrange("p a t -> p (a t)"), bi),
                r=[R_s[bi]], w=[R_b[bi]])
            ph.step("tensor", mk_mms(bi, pb), r=[R_b[bi]], w=[R_ps[bi]])
            ph.step("scalar", mk_stats1(pb), r=[R_ps[bi]], w=[R_st])
            ph.step("vector", var, r=[R_st], w=[R_st])
            ph.step("scalar", rsq, r=[R_st], w=[R_st])

            def norm(e, bi=bi):
                e.tensor_tensor(out=mr[:], in0=m_[:], in1=r_[:], op=OP.mult)
                rb = r_[:].unsqueeze(1).broadcast_to((128, NDM, TL))
                mb = mr[:].unsqueeze(1).broadcast_to((128, NDM, TL))
                e.tensor_tensor(out=t_[:], in0=ssb[:, bi], in1=rb, op=OP.mult)
                e.tensor_tensor(out=t_[:], in0=t_[:], in1=mb, op=OP.subtract)
                if rsb is not None:
                    return e.tensor_tensor(out=u_[:, bi], in0=t_[:], in1=rsb[:, bi],
                                           op=OP.add)
                return e.tensor_copy(out=u_[:, bi], in_=t_[:])
            rlist = [R_st, R_s[bi]] + ([R_x[bi]] if rv is not None else [])
            ph.step("vector", norm, r=rlist, w=[R_u[bi]])
            ph.dma("gpsimd", lambda e, sl=sl, bi=bi: e.dma_start(
                out=ov[:, :, sl], in_=u_[:, bi]), r=[R_u[bi]])
            if also_ln_out is not None:
                ph.step("scalar", cast_from(
                    lambda bi=bi: u_[:, bi].rearrange("p a t -> p (a t)"), bi),
                    r=[R_u[bi]], w=[R_b[bi]])
                ph.step("tensor", mk_mms(bi, pb), r=[R_b[bi]], w=[R_ps[bi]])
                ph.step("scalar", mk_stats1(pb), r=[R_ps[bi]], w=[R_st])
                ph.step("vector", var, r=[R_st], w=[R_st])
                ph.step("scalar", rsq, r=[R_st], w=[R_st])

                def norm2(e, bi=bi):
                    e.tensor_tensor(out=mr[:], in0=m_[:], in1=r_[:], op=OP.mult)
                    rb = r_[:].unsqueeze(1).broadcast_to((128, NDM, TL))
                    mb = mr[:].unsqueeze(1).broadcast_to((128, NDM, TL))
                    e.tensor_tensor(out=t_[:], in0=u_[:, bi], in1=rb, op=OP.mult)
                    return e.tensor_tensor(out=l2[:, bi], in0=t_[:], in1=mb,
                                           op=OP.subtract)
                ph.step("vector", norm2, r=[R_st, R_u[bi]], w=[R_l2[bi]])
                ph.dma("gpsimd", lambda e, sl=sl, bi=bi: e.dma_start(
                    out=lv[:, :, sl], in_=l2[:, bi]), r=[R_l2[bi]])
        ph.run(nc)


def _p6_ffna(nc, v):
    C_S1, C_S3, C_SH1, C_SH3, C_NH1, C_NH3 = 0, 1, 3, 4, 6, 7
    with contextlib.ExitStack() as ps:
        ln2s = ps.enter_context(nc.sbuf_tensor("p6l2", [128, NDM, T], BF16))
        w1raw = ps.enter_context(nc.sbuf_tensor("p6w1", [128, 3, NK * 128], F32))
        w3raw = ps.enter_context(nc.sbuf_tensor("p6w3", [128, 3, NK * 128], F32))
        w1q = ps.enter_context(nc.sbuf_tensor("p6q1", [128, 2, NK * 128], BF16))
        w3q = ps.enter_context(nc.sbuf_tensor("p6q3", [128, 2, NK * 128], BF16))
        qa = ps.enter_context(nc.sbuf_tensor("p6qa", [128, NK * 128], BF16))
        qb = ps.enter_context(nc.sbuf_tensor("p6qb", [128, NK * 128], BF16))
        tsil = ps.enter_context(nc.sbuf_tensor("p6ts", [128, 2, T4], BF16))
        th3 = ps.enter_context(nc.sbuf_tensor("p6t3", [128, 2, T4], BF16))
        grow = ps.enter_context(nc.sbuf_tensor("p6g", [128, 2, T], BF16))
        ph1 = ps.enter_context(nc.psum_tensor("p6p1", [128, 2 * T4], F32))
        ph3 = ps.enter_context(nc.psum_tensor("p6p3", [128, 2 * T4], F32))
        l2v = v.d_ln2.rearrange("(a p) t -> p a t", p=128)
        ph = Pipe("p6")
        R_l2 = ph.res()
        R_w1 = [ph.res(), ph.res(), ph.res()]
        R_w3 = [ph.res(), ph.res(), ph.res()]
        R_q = [ph.res(), ph.res()]
        R_ph = [ph.res(), ph.res()]
        R_ts = [ph.res(), ph.res()]
        R_g = [ph.res(), ph.res()]
        ph.dma("sync", lambda e: e.dma_start(out=ln2s[:], in_=l2v), w=[R_l2])

        def emit_w(h):
            bi = h % 2
            ti = h % 3
            ph.dma("sync", lambda e, h=h, ti=ti: e.dma_start(out=w1raw[:, ti],
                                                             in_=v.d_w1[h]),
                   w=[R_w1[ti]])
            ph.dma("sync", lambda e, h=h, ti=ti: e.dma_start(out=w3raw[:, ti],
                                                             in_=v.d_w3[h]),
                   w=[R_w3[ti]])

            def quant1(e, bi=bi, ti=ti):
                e.tensor_scalar(out=qa[:], in0=w1raw[:, ti],
                                scalar1=v.st_scal[:, C_SH1:C_SH1 + 1],
                                scalar2=None, op0=OP.is_ge)
                e.tensor_scalar(out=qb[:], in0=w1raw[:, ti],
                                scalar1=v.st_scal[:, C_NH1:C_NH1 + 1],
                                scalar2=None, op0=OP.is_lt)
                return e.tensor_tensor(out=w1q[:, bi], in0=qa[:], in1=qb[:],
                                       op=OP.subtract)
            ph.step("vector", quant1, r=[R_w1[ti]], w=[R_q[bi]])

            def quant3(e, bi=bi, ti=ti):
                e.tensor_scalar(out=qa[:], in0=w3raw[:, ti],
                                scalar1=v.st_scal[:, C_SH3:C_SH3 + 1],
                                scalar2=None, op0=OP.is_ge)
                e.tensor_scalar(out=qb[:], in0=w3raw[:, ti],
                                scalar1=v.st_scal[:, C_NH3:C_NH3 + 1],
                                scalar2=None, op0=OP.is_lt)
                return e.tensor_tensor(out=w3q[:, bi], in0=qa[:], in1=qb[:],
                                       op=OP.subtract)
            ph.step("vector", quant3, r=[R_w3[ti]], w=[R_q[bi]])

        emit_w(0)
        for h in range(NH):
            bi = h % 2
            if h + 1 < NH:
                emit_w(h + 1)
            for t4 in range(NT4):
                sl = slice(t4 * T4, (t4 + 1) * T4)
                tp = t4 % 2
                bk = slice(tp * T4, (tp + 1) * T4)

                def mm(e, bi=bi, sl=sl, bk=bk):
                    last = None
                    for k in range(NK):
                        e.matmul(ph1[:, bk], w1q[:, bi, k * 128:(k + 1) * 128],
                                 ln2s[:, k, sl], start=(k == 0), stop=(k == NK - 1))
                        last = e.matmul(ph3[:, bk],
                                        w3q[:, bi, k * 128:(k + 1) * 128],
                                        ln2s[:, k, sl], start=(k == 0),
                                        stop=(k == NK - 1))
                    return last
                ph.step("tensor", mm, r=[R_q[bi], R_l2], w=[R_ph[tp]])

                def evac(e, bk=bk, tp=tp):
                    e.activation(out=tsil[:, tp], in_=ph1[:, bk], func=AF.Silu,
                                 scale=v.st_scal[:, C_S1:C_S1 + 1])
                    return e.activation(out=th3[:, tp], in_=ph3[:, bk], func=AF.Copy,
                                        scale=v.st_scal[:, C_S3:C_S3 + 1])
                ph.step("scalar", evac, r=[R_ph[tp]], w=[R_ts[tp]])
                ph.step("vector", lambda e, sl=sl, tp=tp, bi=bi: e.tensor_tensor(
                    out=grow[:, bi, sl], in0=tsil[:, tp], in1=th3[:, tp],
                    op=OP.mult), r=[R_ts[tp]], w=[R_g[bi]])
            ph.dma("gpsimd", lambda e, h=h, bi=bi: e.dma_start(
                out=v.d_g[h * 128:(h + 1) * 128, :], in_=grow[:, bi]), r=[R_g[bi]])
        ph.run(nc)


def _p7a_ffnb(nc, v):
    C_SH2, C_NH2 = 5, 8
    with contextlib.ExitStack() as ps:
        TH = 1024
        gh = ps.enter_context(nc.sbuf_tensor("p7g", [128, NH, TH], BF16))
        w2raw = ps.enter_context(nc.sbuf_tensor("p7w2", [128, 2, NH * 128], F32))
        w2q = ps.enter_context(nc.sbuf_tensor("p7q2", [128, 2, NH * 128], BF16))
        qa = ps.enter_context(nc.sbuf_tensor("p7qa", [128, NH * 128], BF16))
        qb = ps.enter_context(nc.sbuf_tensor("p7qb", [128, NH * 128], BF16))
        frow = ps.enter_context(nc.sbuf_tensor("p7f", [128, 2, TH], F32))
        pf = ps.enter_context(nc.psum_tensor("p7ps", [128, 2 * T4], F32))
        gv = v.d_g.rearrange("(a p) t -> p a t", p=128)
        ph = Pipe("p7a")
        R_g = ph.res()
        R_wr = [ph.res(), ph.res()]
        R_wq = [ph.res(), ph.res()]
        R_pf = [ph.res(), ph.res()]
        R_fr = [ph.res(), ph.res()]

        def emit_w(dm):
            bi = dm % 2
            ph.dma("sync", lambda e, dm=dm, bi=bi: e.dma_start(
                out=w2raw[:, bi], in_=v.d_w2[dm]), w=[R_wr[bi]])

            def quant(e, bi=bi):
                e.tensor_scalar(out=qa[:], in0=w2raw[:, bi],
                                scalar1=v.st_scal[:, C_SH2:C_SH2 + 1],
                                scalar2=None, op0=OP.is_ge)
                e.tensor_scalar(out=qb[:], in0=w2raw[:, bi],
                                scalar1=v.st_scal[:, C_NH2:C_NH2 + 1],
                                scalar2=None, op0=OP.is_lt)
                return e.tensor_tensor(out=w2q[:, bi], in0=qa[:], in1=qb[:],
                                       op=OP.subtract)
            ph.step("vector", quant, r=[R_wr[bi]], w=[R_wq[bi]])

        emit_w(0)
        for hf in range(2):
            th = slice(hf * TH, (hf + 1) * TH)
            ph.dma("sync", lambda e, th=th: e.dma_start(out=gh[:], in_=gv[:, :, th]),
                   w=[R_g])
            for dm in range(NDM):
                bi = dm % 2
                if hf * NDM + dm + 1 < 2 * NDM:
                    emit_w((dm + 1) % NDM)
                for t4 in range(2):
                    tp = t4 % 2
                    bk = slice(tp * T4, (tp + 1) * T4)

                    def mm(e, bi=bi, t4=t4, bk=bk):
                        last = None
                        for hh in range(NH):
                            last = e.matmul(pf[:, bk],
                                            w2q[:, bi, hh * 128:(hh + 1) * 128],
                                            gh[:, hh, t4 * T4:(t4 + 1) * T4],
                                            start=(hh == 0), stop=(hh == NH - 1))
                        return last
                    ph.step("tensor", mm, r=[R_wq[bi], R_g], w=[R_pf[tp]])
                    ph.step("scalar", lambda e, t4=t4, bk=bk, bi=bi: e.activation(
                        out=frow[:, bi, t4 * T4:(t4 + 1) * T4], in_=pf[:, bk],
                        func=AF.Copy, scale=v.st_scal[:, 2:3]),
                        r=[R_pf[tp]], w=[R_fr[bi]])
                ph.dma("gpsimd", lambda e, dm=dm, hf=hf, bi=bi: e.dma_start(
                    out=v.d_ffn[dm * 128:(dm + 1) * 128, hf * TH:(hf + 1) * TH],
                    in_=frow[:, bi]), r=[R_fr[bi]])
        ph.run(nc)


def _dump_scratch_now(nc, v, name, src_ap, dt_):
    ap = v.dbg.pop(name)
    with contextlib.ExitStack() as ps:
        buf32 = ps.enter_context(nc.sbuf_tensor("dnow32" + name, [128, T], F32))
        buf16 = ps.enter_context(nc.sbuf_tensor("dnow16" + name, [128, T], BF16))
        ph = LS("pd_" + name)
        rows = src_ap.shape[0]
        for r0 in range(0, rows, 128):
            if dt_ is BF16:
                ph.dma("sync", lambda e, r0=r0: e.dma_start(
                    out=buf16[:], in_=src_ap[r0:r0 + 128, :]))
                ph.on("vector", lambda e: e.tensor_copy(out=buf32[:], in_=buf16[:]))
            else:
                ph.dma("sync", lambda e, r0=r0: e.dma_start(
                    out=buf32[:], in_=src_ap[r0:r0 + 128, :]))
                ph.on("vector", lambda e: e.tensor_copy(out=buf32[:], in_=buf32[:]))
            ph.dma("sync", lambda e, r0=r0: e.dma_start(
                out=ap[r0:r0 + 128, :], in_=buf32[:]))
        ph.run(nc)


def _emit_debug(nc, v):
    srcs = {"xc": (v.d_xc32, F32), "sz": (v.d_sz16, BF16), "yg": (v.d_yg16, BF16),
            "mix": (v.d_mix, F32), "x1": (v.d_x1, F32), "ln2": (v.d_ln2, BF16),
            "g": (v.d_g, BF16), "ffn": (v.d_ffn, F32)}
    with contextlib.ExitStack() as ps:
        buf32 = ps.enter_context(nc.sbuf_tensor("dbg32", [128, T], F32))
        buf16 = ps.enter_context(nc.sbuf_tensor("dbg16", [128, T], BF16))
        ph = LS("pdbg")
        for name, ap in v.dbg.items():
            if name in ("ln1", "w1q0", "h1r0", "ts0", "l2s0"):
                continue
            if name == "dtin":
                ph.on("vector", lambda e: e.tensor_copy(out=buf32[:],
                                                        in_=v.dtinT[:]))
                ph.dma("sync", lambda e, ap=ap: e.dma_start(out=ap, in_=buf32[:]))
                continue
            if name == "bc":
                ph.on("vector", lambda e: e.tensor_copy(out=buf32[0:32, :],
                                                        in_=v.bc32[:]))
                ph.dma("sync", lambda e, ap=ap: e.dma_start(out=ap,
                                                            in_=buf32[0:32, :]))
                continue
            if name == "stats":
                def cp(e):
                    e.tensor_copy(out=buf32[:, 0:9], in_=v.st_scal[:])
                    return e.tensor_copy(out=buf32[:, 9:12], in_=v.st_acc[:, 0:3])
                ph.on("vector", cp)
                ph.dma("sync", lambda e, ap=ap: e.dma_start(out=ap,
                                                            in_=buf32[:, 0:16]))
                continue
            src, dt_ = srcs[name]
            rows = src.shape[0]
            for r0 in range(0, rows, 128):
                if dt_ is BF16:
                    ph.dma("sync", lambda e, src=src, r0=r0: e.dma_start(
                        out=buf16[:], in_=src[r0:r0 + 128, :]))
                    ph.on("vector", lambda e: e.tensor_copy(out=buf32[:],
                                                            in_=buf16[:]))
                else:
                    ph.dma("sync", lambda e, src=src, r0=r0: e.dma_start(
                        out=buf32[:], in_=src[r0:r0 + 128, :]))
                    ph.on("vector", lambda e: e.tensor_copy(out=buf32[:],
                                                            in_=buf32[:]))
                ph.dma("sync", lambda e, ap=ap, r0=r0: e.dma_start(
                    out=ap[r0:r0 + 128, :], in_=buf32[:]))
        ph.run(nc)



def _pretile(inp):
    f = np.ascontiguousarray
    W_in = inp["L0_attn_W_in"]
    w_in_t = f(W_in.reshape(NK, 128, 64, 128).transpose(2, 1, 0, 3)
               .reshape(64, 128, NK * 128))
    conv_w = inp["L0_attn_conv_w"][:, 0, :]
    conv_w_t = f(np.concatenate(
        [conv_w[:, j].reshape(NCC, 128).T for j in range(3)], axis=1))
    conv_b_t = f(inp["L0_attn_conv_b"].reshape(NCC, 128).T)
    w_x_t = f(inp["L0_attn_W_x"].reshape(NCC, 128, 160))
    w_dt_t = f(inp["L0_attn_W_dt"])
    b_dt_t = f(inp["L0_attn_b_dt"].reshape(NCC, 128).T)
    a_log_t = f(inp["L0_attn_A_log"].reshape(NCC, 128, S).transpose(1, 0, 2)
                .reshape(128, NCC * S))
    d_t = f(inp["L0_attn_D"].reshape(NCC, 128).T)
    w_out_t = f(inp["L0_attn_W_out"].reshape(NCC, 128, NDM, 128)
                .transpose(2, 1, 0, 3).reshape(NDM, 128, NCC * 128))
    w1 = np.zeros((HP, DM), np.float32); w1[:H_TRUE] = inp["L0_ffn_w1"]
    w3 = np.zeros((HP, DM), np.float32); w3[:H_TRUE] = inp["L0_ffn_w3"]
    w2 = np.zeros((DM, HP), np.float32); w2[:, :H_TRUE] = inp["L0_ffn_w2"]
    w1_t = f(w1.T.reshape(NK, 128, NH, 128).transpose(2, 1, 0, 3)
             .reshape(NH, 128, NK * 128))
    w3_t = f(w3.T.reshape(NK, 128, NH, 128).transpose(2, 1, 0, 3)
             .reshape(NH, 128, NK * 128))
    w2_t = f(w2.T.reshape(NH, 128, NDM, 128).transpose(2, 1, 0, 3)
             .reshape(NDM, 128, NH * 128))
    es_const = np.zeros((32, 32 * 128), np.float32)
    for s in range(32):
        es_const[s, s * 128:(s + 1) * 128] = 1.0
    return dict(w_in_t=w_in_t, conv_w_t=conv_w_t, conv_b_t=conv_b_t,
                w_x_t=w_x_t, w_dt_t=w_dt_t, b_dt_t=b_dt_t, a_log_t=a_log_t,
                d_t=d_t, w_out_t=w_out_t, w1_t=w1_t, w3_t=w3_t, w2_t=w2_t,
                es_const=es_const)


def make_in_maps(inputs):
    x = np.asarray(inputs["L0_attn_x"], dtype=np.float32)
    shared = _pretile({k: np.asarray(v_, dtype=np.float32)
                       for k, v_ in inputs.items() if k != "L0_attn_x"})
    in_maps = []
    for b in range(B):
        m = dict(shared)
        m["xT"] = np.ascontiguousarray(x[b].T)
        in_maps.append(m)
    return in_maps


def kernel(**inputs):
    in_maps = make_in_maps(inputs)
    nc = build()
    res = run_bass_kernel_spmd(nc, in_maps, core_ids=list(range(B)))
    out = np.empty((B, T, DM), np.float32)
    for b in range(B):
        out[b] = res.results[b]["o"].T
    return out
